# version 17 (final)
# baseline (speedup 1.0000x reference)
import math
import contextlib
import numpy as np
import ml_dtypes
import concourse.bass as bass
import concourse.mybir as mybir
from concourse.bass_utils import run_bass_kernel_spmd

F32 = mybir.dt.float32
BF16 = mybir.dt.bfloat16
I32 = mybir.dt.int32
AF = mybir.ActivationFunctionType
ALU = mybir.AluOpType
AX = mybir.AxisListType

S = 4096
SL = 2048
D = 2048
DFF = 8192
NPAIR = 16
NSLOT = 384
SCALE = 1.0 / math.sqrt(32.0)
NBIS = 10
EPS = 1e-5


def build_nc(upto=99, dbg=False):
    nc = bass.Bass("TRN2", target_bir_lowering=False,
                   detect_race_conditions=False)

    x = nc.dram_tensor("x", [S, D], F32, kind="ExternalInput").ap()
    wq = nc.dram_tensor("wq", [D, D], F32, kind="ExternalInput").ap()
    wk = nc.dram_tensor("wk", [D, D], F32, kind="ExternalInput").ap()
    wv = nc.dram_tensor("wv", [D, D], F32, kind="ExternalInput").ap()
    wo = nc.dram_tensor("wo", [D, D], F32, kind="ExternalInput").ap()
    bq = nc.dram_tensor("bq", [D], F32, kind="ExternalInput").ap()
    bk = nc.dram_tensor("bk", [D], F32, kind="ExternalInput").ap()
    bv = nc.dram_tensor("bv", [D], F32, kind="ExternalInput").ap()
    bo = nc.dram_tensor("bo", [D], F32, kind="ExternalInput").ap()
    wr = nc.dram_tensor("wr", [D, 1], F32, kind="ExternalInput").ap()
    br = nc.dram_tensor("br", [1], F32, kind="ExternalInput").ap()
    w1 = nc.dram_tensor("w1", [D, DFF], F32, kind="ExternalInput").ap()
    b1 = nc.dram_tensor("b1", [DFF], F32, kind="ExternalInput").ap()
    w2 = nc.dram_tensor("w2", [DFF, D], F32, kind="ExternalInput").ap()
    b2 = nc.dram_tensor("b2", [D], F32, kind="ExternalInput").ap()
    out = nc.dram_tensor("out", [SL, D], F32, kind="ExternalOutput").ap()

    xnd = nc.dram_tensor("xnd", [S, D], BF16, kind="Internal").ap()
    qtd = nc.dram_tensor("qtd", [D, SL], BF16, kind="Internal").ap()
    ktd = nc.dram_tensor("ktd", [D, S], BF16, kind="Internal").ap()
    vtd = nc.dram_tensor("vtd", [D, S], BF16, kind="Internal").ap()
    ptd = nc.dram_tensor("ptd", [128, 3 * 16 * 128], BF16, kind="Internal").ap()

    A = nc.alloc_sbuf_tensor
    SEM = nc.alloc_semaphore

    idn = A("idn", [128, 128], BF16)
    idn_f = A("idn_f", [128, 128], F32)
    ltri = A("ltri", [128, 128], BF16)
    ones_col = A("ones_col", [128, 1], BF16)
    ones_rf = A("ones_rf", [1, 128], F32)

    mean_all = A("mean_all", [128, 32], F32)
    rstd_all = A("rstd_all", [128, 32], F32)
    nb_all = A("nb_all", [128, 32], F32)
    w_all = A("w_all", [128, 32], F32)
    mv = [A(f"mv{i}", [128, 2], F32) for i in range(2)]
    std_c = [A(f"std{i}", [128, 1], F32) for i in range(2)]

    br_col = A("br_col", [128, 1], F32)
    bqc = A("bqc", [128, 16], F32)
    bkc = A("bkc", [128, 16], F32)
    bvc = A("bvc", [128, 16], F32)
    b1c = A("b1c", [128, 64], F32)

    lo_c = A("lo_c", [128, 1], F32)
    hi_c = A("hi_c", [128, 1], F32)
    mid_c = A("mid_c", [128, 1], F32)
    cnd_c = A("cnd_c", [128, 1], F32)
    d1_c = A("d1_c", [128, 1], F32)
    d2_c = A("d2_c", [128, 1], F32)
    cmp_t = A("cmp_t", [128, 32], F32)
    part_c = A("part_c", [128, 1], F32)
    part_b = A("part_b", [128, 1], BF16)
    cnt_sb = A("cnt_sb", [1, 1], F32)

    mask_t = A("mask_t", [128, 16], F32)
    notm_t = A("notm_t", [128, 16], F32)
    a1_t = A("a1_t", [128, 16], F32)
    b1p_t = A("b1p_t", [128, 16], F32)
    gw_t = A("gw_t", [128, 16], F32)
    incl_t = A("incl_t", [128, 16], F32)
    rank_t = A("rank_t", [128, 16], F32)
    rtot_b = A("rtot_b", [128, 1], BF16)
    offs_c = A("offs_c", [128, 1], F32)
    zeros16 = A("zeros16", [128, 16], F32)
    eps_col = A("eps_col", [128, 1], F32)

    late = {}

    pA = [nc.alloc_psum_tensor(f"pA{i}", [128, 1024], F32) for i in range(3)]
    pB0 = nc.alloc_psum_tensor("pB0", [128, 512], F32)

    s_gio = SEM("s_gio"); s_cst = SEM("s_cst"); s_cb = SEM("s_cb")
    s_ct = SEM("s_ct"); s_x = SEM("s_x"); s_d1 = SEM("s_d1")
    s_a1 = SEM("s_a1"); s_d2 = SEM("s_d2"); s_a2 = SEM("s_a2")
    s_d3 = SEM("s_d3"); s_xnst = SEM("s_xnst"); s_wl = SEM("s_wl")
    s_tr = SEM("s_tr"); s_qm = SEM("s_qm"); s_qe = SEM("s_qe")
    s_ks = SEM("s_ks"); s_xf = SEM("s_xf")
    s_bm = SEM("s_bm"); s_bd = SEM("s_bd"); s_bp = SEM("s_bp")
    s_ba = SEM("s_ba"); s_bc = SEM("s_bc")
    s_m1 = SEM("s_m1"); s_pm = SEM("s_pm"); s_pa = SEM("s_pa"); s_ptd = SEM("s_ptd")
    s_gl = SEM("s_gl")
    s_f1 = SEM("s_f1"); s_f2 = SEM("s_f2"); s_f3 = SEM("s_f3")
    s_f4 = SEM("s_f4"); s_f5 = SEM("s_f5")
    s_pl = SEM("s_pl"); s_pl2 = SEM("s_pl2"); s_sc = SEM("s_sc")
    s_ex = SEM("s_ex"); s_av = SEM("s_av"); s_rc = SEM("s_rc")
    s_bcm = SEM("s_bcm"); s_fe = SEM("s_fe"); s_fn = SEM("s_fn")
    s_x2 = SEM("s_x2"); s_om = SEM("s_om"); s_oa = SEM("s_oa")
    s_bb = SEM("s_bb"); s_fl = SEM("s_fl"); s_fla = SEM("s_fla")
    flc = [0]
    flca = [0]

    def VF(v, inst):
        inst.then_inc(s_fl, 1)
        flc[0] += 1
        v.wait_ge(s_fl, flc[0])

    def SF(s, inst):
        inst.then_inc(s_fla, 1)
        flca[0] += 1
        s.wait_ge(s_fla, flca[0])
    s_os = SEM("s_os")

    es = contextlib.ExitStack()
    blk = es.enter_context(nc.Block())

    def ph_consts(st):
        scr_i = st.enter_context(nc.sbuf_tensor("scr_i", [128, NSLOT], I32))
        scr_f = st.enter_context(nc.sbuf_tensor("scr_f", [128, 128], F32))
        rowv = st.enter_context(nc.sbuf_tensor("rowv", [128, 1], F32))
        btile = st.enter_context(nc.sbuf_tensor("btile", [128, 128], F32))
        @blk.gpsimd
        def _(g):
            g.iota(scr_i[:, 0:128], pattern=[[1, 128]], base=0,
                   channel_multiplier=0).then_inc(s_gio, 1)
            g.iota(scr_i[:, 128:129], pattern=[[0, 1]], base=0,
                   channel_multiplier=1).then_inc(s_gio, 1)

        @blk.sync
        def _(sync):
            sync.dma_start(out=br_col[:], in_=br.partition_broadcast(128)).then_inc(s_cst, 16)
            sync.dma_start(out=btile[0:16, :], in_=bq.rearrange("(c p) -> c p", p=128)).then_inc(s_cst, 16)
            sync.dma_start(out=btile[16:32, :], in_=bk.rearrange("(c p) -> c p", p=128)).then_inc(s_cst, 16)
            sync.dma_start(out=btile[32:48, :], in_=bv.rearrange("(c p) -> c p", p=128)).then_inc(s_cst, 16)

        @blk.vector
        def _(v):
            v.wait_ge(s_gio, 2)
            v.tensor_copy(out=rowv[:], in_=scr_i[:, 128:129])
            VF(v, v.tensor_copy(out=scr_f[:], in_=scr_i[:, 0:128]))
            v.nop().then_inc(s_cb, 1)
            VF(v, v.tensor_scalar(out=idn[:], in0=scr_f[:], scalar1=rowv[:],
                                  scalar2=None, op0=ALU.is_equal))
            v.tensor_scalar(out=ltri[:], in0=scr_f[:], scalar1=rowv[:],
                            scalar2=None, op0=ALU.is_gt)
            v.tensor_copy(out=idn_f[:], in_=idn[:])
            v.memset(ones_col[:], 1.0)
            v.memset(ones_rf[:], 1.0)
            v.memset(zeros16[:], 0.0)
            v.memset(eps_col[:], EPS).then_inc(s_cb, 1)

        @blk.tensor
        def _(t):
            t.wait_ge(s_cb, 2)
            t.wait_ge(s_cst, 16 * 4)
            t.transpose(out=pB0[:, 0:48], in_=btile[0:48, :],
                        identity=idn_f[0:48, 0:48]).then_inc(s_ct, 1)

        @blk.scalar
        def _(s):
            s.wait_ge(s_ct, 1)
            s.copy(out=bqc[:], in_=pB0[:, 0:16])
            s.copy(out=bkc[:], in_=pB0[:, 16:32])
            s.copy(out=bvc[:], in_=pB0[:, 32:48]).then_inc(s_pa, 1)

        @blk.sync
        def _(sync):
            sync.wait_ge(s_pa, 1)
            sync.dma_start(out=btile[0:64, :], in_=b1.rearrange("(c p) -> c p", p=128)).then_inc(s_cst, 16)

        @blk.tensor
        def _(t):
            t.wait_ge(s_cst, 16 * 5)
            t.transpose(out=pB0[:, 64:128], in_=btile[0:64, :],
                        identity=idn_f[0:64, 0:64]).then_inc(s_ct, 1)

        @blk.scalar
        def _(s):
            s.wait_ge(s_ct, 2)
            s.copy(out=b1c[:], in_=pB0[:, 64:128]).then_inc(s_pa, 1)

    def ph_p0(st):
        xt = [st.enter_context(nc.sbuf_tensor(f"xt{i}", [128, D], F32)) for i in range(2)]
        xnf = [st.enter_context(nc.sbuf_tensor(f"xnf{i}", [128, D], F32)) for i in range(2)]
        xnb = [st.enter_context(nc.sbuf_tensor(f"xnb{i}", [128, D], BF16)) for i in range(2)]
        junk = st.enter_context(nc.sbuf_tensor("junk", [128, D], BF16))
        wr_bc = st.enter_context(nc.sbuf_tensor("wr_bc", [128, D], F32))
        st24 = st.enter_context(nc.sbuf_tensor("st24", [128, 24], F32))

        @blk.sync
        def _(sync):
            sync.dma_start(out=wr_bc[:],
                           in_=wr.rearrange("a b -> (a b)").partition_broadcast(128)).then_inc(s_cst, 16)
            for t in range(32):
                if t >= 2:
                    sync.wait_ge(s_d1, t - 1)
                    sync.wait_ge(s_a2, t - 1)
                sync.dma_start(out=xt[t % 2][:],
                               in_=x[t * 128:(t + 1) * 128, :]).then_inc(s_x, 16)
                if t >= 2:
                    sync.wait_ge(s_d3, t - 1)
                    sync.dma_start(out=xnd[(t - 2) * 128:(t - 1) * 128, :],
                                   in_=xnb[t % 2][:]).then_inc(s_xnst, 16)
            for t in range(30, 32):
                sync.wait_ge(s_d3, t + 1)
                sync.dma_start(out=xnd[t * 128:(t + 1) * 128, :],
                               in_=xnb[t % 2][:]).then_inc(s_xnst, 16)

        @blk.vector
        def _(v):
            v.wait_ge(s_cst, 16 * 6)
            v.wait_ge(s_cb, 2)
            for t in range(32):
                v.wait_ge(s_x, 16 * (t + 1))
                for i in range(3):
                    v.bn_stats(out=st24[:, 6 * i:6 * i + 6],
                               in_=xt[t % 2][:, 512 * i:512 * (i + 1)])
                VF(v, v.bn_stats(out=st24[:, 18:24], in_=xt[t % 2][:, 1536:2048]))
                v.bn_aggr(out=mv[t % 2][:], in_=st24[:]).then_inc(s_d1, 1)
                v.wait_ge(s_a1, t + 1)
                VF(v, v.reciprocal(out=rstd_all[:, t:t + 1], in_=std_c[t % 2][:]))
                v.tensor_copy(out=mean_all[:, t:t + 1], in_=mv[t % 2][:, 0:1])
                v.scalar_tensor_tensor(out=nb_all[:, t:t + 1], in0=mv[t % 2][:, 0:1],
                                       scalar=-1.0, in1=rstd_all[:, t:t + 1],
                                       op0=ALU.mult, op1=ALU.mult).then_inc(s_d2, 1)
                v.wait_ge(s_a2, t + 1)
                if t >= 2:
                    v.wait_ge(s_xnst, 16 * (t - 1))
                v.scalar_tensor_tensor(out=junk[:], in0=xnf[t % 2][:], scalar=1.0,
                                       in1=wr_bc[:], op0=ALU.mult, op1=ALU.mult,
                                       accum_out=w_all[:, t:t + 1])
                v.tensor_copy(out=xnb[t % 2][:], in_=xnf[t % 2][:]).then_inc(s_d3, 1)
            v.wait_ge(s_d3, 32)
            v.tensor_scalar(out=w_all[:], in0=w_all[:], scalar1=br_col[:],
                            scalar2=None, op0=ALU.add).then_inc(s_d3, 1)

        @blk.scalar
        def _(s):
            for t in range(32):
                s.wait_ge(s_d1, t + 1)
                s.activation(out=std_c[t % 2][:], in_=mv[t % 2][:, 1:2],
                             func=AF.Sqrt, bias=eps_col[:]).then_inc(s_a1, 1)
                s.wait_ge(s_d2, t + 1)
                s.wait_ge(s_x, 16 * (t + 1))
                if t >= 2:
                    s.wait_ge(s_d3, t - 1)
                s.activation(out=xnf[t % 2][:], in_=xt[t % 2][:], func=AF.Identity,
                             bias=nb_all[:, t:t + 1],
                             scale=rstd_all[:, t:t + 1]).then_inc(s_a2, 1)


    def ph_p2(st, early):
        xnt = early["xnt"]
        kst = early["kst"]

        def tr_chunk(eng, n_src, buf):
            for c in range(16):
                eng.dma_start(out=xnt[buf][:, c, :],
                              in_=xnd[n_src * 512:(n_src + 1) * 512,
                                      c * 128:(c + 1) * 128],
                              transpose=True).then_inc(s_tr, 16)

        def q_store(sync, n):
            for m in range(16):
                idx = n * 16 + m
                sync.wait_ge(s_qe, idx + 1)
                sync.dma_start(out=qtd[m * 128:(m + 1) * 128, n * 512:(n + 1) * 512],
                               in_=kst[idx % 16][:]).then_inc(s_ks, 16)

        def kv_store(sync, i, n):
            for j in range(32):
                idx = 64 + i * 32 + j
                m = j if j < 16 else j - 16
                dst = ktd if j < 16 else vtd
                sync.wait_ge(s_qe, idx + 1)
                sync.dma_start(out=dst[m * 128:(m + 1) * 128, n * 512:(n + 1) * 512],
                               in_=kst[idx % 16][:]).then_inc(s_ks, 16)

        with contextlib.ExitStack() as st2:
            wqs = early["wqs"]

            @blk.gpsimd
            def _(g):
                g.dma_start(out=wqs[:],
                            in_=wq.rearrange("(c p) d -> p c d", p=128)).then_inc(s_wl, 16)
                g.iota(early["iota_i"][:], pattern=[[1, NSLOT]], base=0,
                       channel_multiplier=0).then_inc(s_gio, 1)

            @blk.sync
            def _(sync):
                sync.wait_ge(s_xnst, 16 * 32)
                for n in range(4):
                    if n >= 2:
                        sync.wait_ge(s_qm, 16 * (n - 1))
                    tr_chunk(sync, n, n % 2)
                    if n >= 1:
                        q_store(sync, n - 1)

            def q_evac(s, n):
                for m in range(16):
                    idx = n * 16 + m
                    s.wait_ge(s_qm, idx + 1)
                    if idx >= 16:
                        s.wait_ge(s_ks, 16 * (idx - 15))
                    s.activation(out=kst[idx % 16][:], in_=pA[idx % 3][:, 0:512],
                                 func=AF.Identity, bias=bqc[:, m:m + 1],
                                 scale=1.0).then_inc(s_qe, 1)

            @blk.scalar
            def _(s):
                for n in range(4):
                    q_evac(s, n)

            @blk.tensor
            def _(t):
                t.wait_ge(s_wl, 16)
                for n in range(4):
                    t.wait_ge(s_tr, 256 * (n + 1))
                    for m in range(16):
                        idx = n * 16 + m
                        pt = pA[idx % 3][:, 0:512]
                        if idx >= 3:
                            t.wait_ge(s_qe, idx - 2)
                        for c in range(16):
                            mm = t.matmul(out=pt, lhsT=wqs[:, c, m * 128:(m + 1) * 128],
                                          rhs=xnt[n % 2][:, c, :],
                                          start=(c == 0), stop=(c == 15))
                        mm.then_inc(s_qm, 1)

            @blk.sync
            def _(sync):
                q_store(sync, 3)

        KVORD = [2, 3, 0, 1, 4, 5, 6, 7]
        with contextlib.ExitStack() as st2:
            early["stq"].close()
            wks = st2.enter_context(nc.sbuf_tensor("wks", [128, 16, 2048], BF16))
            wvs = st2.enter_context(nc.sbuf_tensor("wvs", [128, 16, 2048], BF16))

            @blk.gpsimd
            def _(g):
                g.wait_ge(s_qm, 64)
                g.wait_ge(s_d3, 33)
                for wsl, dst in ((wk, wks), (wv, wvs)):
                    for hh in range(2):
                        g.dma_start(out=dst[:, :, hh * 1024:(hh + 1) * 1024],
                                    in_=wsl.rearrange("(c p) d -> p c d", p=128)
                                    [:, :, hh * 1024:(hh + 1) * 1024]).then_inc(s_wl, 16)

            @blk.sync
            def _(sync):
                for i, n in enumerate(KVORD):
                    if i >= 2:
                        sync.wait_ge(s_qm, 64 + 32 * (i - 1))
                        tr_chunk(sync, n, i % 2)
                    if i >= 1:
                        kv_store(sync, i - 1, KVORD[i - 1])

            cnt_ps = pB0[0:1, 0:1]
            cb_ps = pB0[0:128, 256:257]
            mbc_ps = pB0[0:128, 16:32]
            c16_ps = pB0[0:1, 300:316]

            def srch_act(s, k):
                if k == 4:
                    s.wait_ge(s_bp, 1)
                    SF(s, s.activation(out=cnt_sb[:], in_=cnt_ps, func=AF.Sqrt,
                                       scale=1.0))
                    SF(s, s.activation(out=cnt_sb[:], in_=cnt_sb[:], func=AF.Sqrt,
                                       scale=1.0))
                    s.activation(out=cnt_sb[:], in_=cnt_sb[:], func=AF.Sqrt,
                                 scale=1.0).then_inc(s_ba, 1)
                elif k >= 8 and k < 8 + 9 * NBIS and (k - 8) % 9 == 0:
                    r = (k - 8) // 9
                    s.wait_ge(s_bp, r + 3)
                    s.copy(out=early["mbs"][:], in_=mbc_ps).then_inc(s_ba, 1)

            def kv_evac(s, i):
                for j in range(32):
                    idx = 64 + i * 32 + j
                    m = j if j < 16 else j - 16
                    bias = bkc if j < 16 else bvc
                    s.wait_ge(s_qm, idx + 1)
                    s.wait_ge(s_ks, 16 * (idx - 15))
                    s.activation(out=kst[idx % 16][:], in_=pA[idx % 3][:, 0:512],
                                 func=AF.Identity, bias=bias[:, m:m + 1],
                                 scale=1.0).then_inc(s_qe, 1)
                    srch_act(s, i * 32 + j)

            @blk.scalar
            def _(s):
                for i in range(8):
                    kv_evac(s, i)

            def srch_pe(t, k):
                if k == 2:
                    t.wait_ge(s_bd, 1)
                    t.matmul(out=cnt_ps, lhsT=part_b[:], rhs=ones_col[:],
                             start=True, stop=True,
                             skip_group_check=True).then_inc(s_bp, 1)
                elif k >= 6 and k < 6 + 9 * NBIS and (k - 6) % 9 == 0:
                    r = (k - 6) // 9
                    t.wait_ge(s_bm, r + 1)
                    t.matmul(out=mbc_ps, lhsT=ones_rf[:], rhs=early["mrow"][:],
                             start=True, stop=True,
                             skip_group_check=True).then_inc(s_bp, 2 if r == 0 else 1)
                elif k >= 12 and k < 12 + 9 * NBIS and (k - 12) % 9 == 0:
                    r = (k - 12) // 9
                    t.wait_ge(s_bd, r + 2)
                    t.matmul(out=c16_ps, lhsT=ones_col[:], rhs=early["p16b"][:],
                             start=True, stop=True,
                             skip_group_check=True).then_inc(s_bc, 1)
                elif k == 9 * NBIS + 6:
                    t.wait_ge(s_bm, NBIS + 1)
                    t.matmul(out=cb_ps, lhsT=ones_rf[:], rhs=early["hir"][:],
                             start=True, stop=True,
                             skip_group_check=True).then_inc(s_bp, 1)
                elif k == 9 * NBIS + 18:
                    t.wait_ge(s_m1, 2)
                    t.matmul(out=cb_ps, lhsT=ltri[:], rhs=rtot_b[:], start=True,
                             stop=True, skip_group_check=True).then_inc(s_pm, 1)

            @blk.tensor
            def _(t):
                for i, n in enumerate(KVORD):
                    if i >= 2:
                        t.wait_ge(s_tr, 256 * (i + 3))
                    for j in range(32):
                        idx = 64 + i * 32 + j
                        m = j if j < 16 else j - 16
                        ws = wks if j < 16 else wvs
                        if i == 0:
                            t.wait_ge(s_wl, (32 if j < 16 else 64) +
                                      (0 if m < 8 else 16))
                        srch_pe(t, i * 32 + j)
                        pt = pA[idx % 3][:, 0:512]
                        if idx >= 3:
                            t.wait_ge(s_qe, idx - 2)
                        for c in range(16):
                            mm = t.matmul(out=pt, lhsT=ws[:, c, m * 128:(m + 1) * 128],
                                          rhs=xnt[i % 2][:, c, :],
                                          start=(c == 0), stop=(c == 15))
                        mm.then_inc(s_qm, 1)

            @blk.sync
            def _(sync):
                kv_store(sync, 7, KVORD[7])

    def ph_p2b(st, early):
        psb = st.enter_context(nc.sbuf_tensor("psb", [128, 16, NSLOT], BF16))
        iota384 = early["iota384"]
        iota1r = early["iota1r"]
        mrow = early["mrow"]
        brow = early["brow"]
        lor = early["lor"]
        hir = early["hir"]
        hrow = early["hrow"]
        t16a = early["t16a"]
        t16b = early["t16b"]
        s1r = early["s1r"]
        s2r = early["s2r"]
        q1r = early["q1r"]
        q2r = early["q2r"]
        mbs = early["mbs"]
        cscr = early["cscr"]
        p16 = early["p16"]
        p16b = early["p16b"]
        iota_i = early["iota_i"]
        psbf = st.enter_context(nc.sbuf_tensor("psbf", [128, 16, NSLOT], F32))
        xloc = st.enter_context(nc.sbuf_tensor("xloc", [128, 16, 2048], BF16))
        ptp = st.enter_context(nc.psum_tensor("ptp", [128, 128], BF16))

        cnt_ps = pB0[0:1, 0:1]
        cb_ps = pB0[0:128, 256:257]
        mbc_ps = pB0[0:128, 16:32]
        c16_ps = pB0[0:1, 300:316]

        @blk.vector
        def _(v):
            v.wait_ge(s_gio, 3)
            v.tensor_copy(out=iota384[:], in_=iota_i[:])
            VF(v, v.tensor_scalar(out=iota1r[:], in0=iota384[0:1, 0:16],
                                  scalar1=1.0, scalar2=None, op0=ALU.add))
            v.wait_ge(s_d3, 33)
            VF(v, v.tensor_tensor(out=cmp_t[:], in0=w_all[:], in1=w_all[:], op=ALU.mult))
            VF(v, v.tensor_tensor(out=cmp_t[:], in0=cmp_t[:], in1=cmp_t[:], op=ALU.mult))
            VF(v, v.tensor_tensor(out=cmp_t[:], in0=cmp_t[:], in1=cmp_t[:], op=ALU.mult))
            VF(v, v.reduce_sum(out=part_c[:], in_=cmp_t[:], axis=AX.X))
            v.tensor_copy(out=part_b[:], in_=part_c[:]).then_inc(s_bd, 1)


        @blk.vector
        def _(v):
            v.wait_ge(s_ba, 1)
            v.memset(brow[0:1, 0:1], 1.0)
            v.memset(brow[0:1, 17:18], 0.0)
            VF(v, v.tensor_copy(out=hir[:], in_=cnt_sb[:]))
            VF(v, v.tensor_scalar(out=lor[:], in0=hir[:], scalar1=-1.0,
                                  scalar2=None, op0=ALU.mult))
            VF(v, v.tensor_scalar(out=hrow[:], in0=hir[:], scalar1=2.0 / 17.0,
                                  scalar2=None, op0=ALU.mult))
            v.tensor_scalar(out=mrow[:], in0=iota1r[:], scalar1=hrow[:],
                            scalar2=lor[:], op0=ALU.mult,
                            op1=ALU.add).then_inc(s_bm, 1)

        for r in range(NBIS):
          with nc.named_scope(f"P2b_srch{r}"):
              @blk.vector
              def _(v, r=r):
                  v.wait_ge(s_ba, r + 2)
                  VF(v, v.tensor_tensor(
                      out=cscr[:],
                      in0=w_all[:].rearrange("p (a f) -> p a f", a=1).broadcast_to([128, 16, 32]),
                      in1=mbs[:].rearrange("p (f a) -> p f a", a=1).broadcast_to([128, 16, 32]),
                      op=ALU.is_lt))
                  VF(v, v.tensor_reduce(out=p16[:], in_=cscr[:], axis=AX.X, op=ALU.add))
                  v.tensor_copy(out=p16b[:], in_=p16[:]).then_inc(s_bd, 1)

              @blk.tensor
              def _(t, r=r):
                  t.wait_ge(s_bd, r + 2)
                  t.matmul(out=c16_ps, lhsT=ones_col[:], rhs=p16b[:],
                           start=True, stop=True,
                           skip_group_check=True).then_inc(s_bc, 1)

              @blk.vector
              def _(v, r=r):
                  v.wait_ge(s_bc, r + 1)
                  VF(v, v.tensor_single_scalar(out=brow[0:1, 1:17], in_=c16_ps,
                                               scalar=3584.5, op=ALU.is_lt))
                  v.tensor_tensor(out=t16a[:], in0=brow[0:1, 1:17],
                                  in1=brow[0:1, 2:18], op=ALU.subtract)
                  VF(v, v.tensor_tensor(out=t16b[:], in0=brow[0:1, 0:16],
                                        in1=brow[0:1, 1:17], op=ALU.subtract))
                  v.tensor_tensor(out=t16a[:], in0=t16a[:], in1=mrow[:], op=ALU.mult)
                  VF(v, v.tensor_tensor(out=t16b[:], in0=t16b[:], in1=mrow[:], op=ALU.mult))
                  v.tensor_reduce(out=s1r[:], in_=t16a[:], axis=AX.X, op=ALU.add)
                  VF(v, v.tensor_reduce(out=s2r[:], in_=t16b[:], axis=AX.X, op=ALU.add))
                  v.tensor_scalar(out=q1r[:], in0=brow[0:1, 1:2], scalar1=-1.0,
                                  scalar2=1.0, op0=ALU.mult, op1=ALU.add)
                  VF(v, v.tensor_tensor(out=q2r[:], in0=brow[0:1, 16:17],
                                        in1=hir[:], op=ALU.mult))
                  VF(v, v.tensor_tensor(out=q1r[:], in0=q1r[:], in1=lor[:], op=ALU.mult))
                  VF(v, v.tensor_tensor(out=lor[:], in0=s1r[:], in1=q1r[:], op=ALU.add))
                  VF(v, v.tensor_tensor(out=hir[:], in0=s2r[:], in1=q2r[:], op=ALU.add))
                  if r < NBIS - 1:
                      VF(v, v.tensor_tensor(out=hrow[:], in0=hir[:], in1=lor[:],
                                            op=ALU.subtract))
                      VF(v, v.tensor_scalar(out=hrow[:], in0=hrow[:], scalar1=1.0 / 17.0,
                                            scalar2=None, op0=ALU.mult))
                      v.tensor_scalar(out=mrow[:], in0=iota1r[:], scalar1=hrow[:],
                                      scalar2=lor[:], op0=ALU.mult,
                                      op1=ALU.add).then_inc(s_bm, 1)
                  else:
                      v.nop().then_inc(s_bm, 1)

        @blk.vector
        def _(v):
            v.wait_ge(s_bp, NBIS + 3)
            v.tensor_copy(out=hi_c[:], in_=cb_ps).then_inc(s_m1, 1)
        @blk.vector
        def _(v):
            v.wait_ge(s_m1, 1)
            VF(v, v.tensor_scalar(out=mask_t[:], in0=w_all[:, 0:16], scalar1=hi_c[:],
                                  scalar2=None, op0=ALU.is_gt))
            VF(v, v.tensor_scalar(out=notm_t[:], in0=mask_t[:], scalar1=-1.0, scalar2=1.0,
                                  op0=ALU.mult, op1=ALU.add))
            v.tensor_tensor(out=a1_t[:], in0=rstd_all[:, 0:16], in1=notm_t[:], op=ALU.mult)
            v.tensor_tensor(out=b1p_t[:], in0=nb_all[:, 0:16], in1=notm_t[:], op=ALU.mult)
            v.tensor_tensor(out=gw_t[:], in0=w_all[:, 0:16], in1=mask_t[:], op=ALU.mult)
            VF(v, v.tensor_tensor_scan(out=incl_t[:], data0=mask_t[:], data1=zeros16[:],
                                       initial=0.0, op0=ALU.add, op1=ALU.add))
            v.tensor_scalar(out=a1_t[:], in0=a1_t[:], scalar1=1.0, scalar2=None, op0=ALU.add)
            VF(v, v.tensor_tensor(out=rank_t[:], in0=incl_t[:], in1=mask_t[:], op=ALU.subtract))
            VF(v, v.reduce_sum(out=part_c[:], in_=mask_t[:], axis=AX.X))
            v.tensor_copy(out=rtot_b[:], in_=part_c[:]).then_inc(s_m1, 1)

        @blk.vector
        def _(v):
            v.wait_ge(s_pm, 1)
            v.tensor_copy(out=offs_c[:], in_=cb_ps).then_inc(s_m1, 1)
            v.wait_ge(s_m1, 3)
            v.tensor_scalar(out=rank_t[:], in0=rank_t[:], scalar1=offs_c[:],
                            scalar2=None, op0=ALU.add).then_inc(s_m1, 1)
            v.wait_ge(s_m1, 4)
            v.wait_ge(s_ks, 16 * 320)
            for t_ in range(15):
                v.tensor_scalar(out=psbf[:, t_, :], in0=iota384[:],
                                scalar1=rank_t[:, t_:t_ + 1], scalar2=None,
                                op0=ALU.is_equal)
            VF(v, v.tensor_scalar(out=psbf[:, 15, :], in0=iota384[:],
                                  scalar1=rank_t[:, 15:16], scalar2=None,
                                  op0=ALU.is_equal))
            for t_ in range(16):
                inst = v.tensor_scalar(out=psbf[:, t_, :], in0=psbf[:, t_, :],
                                       scalar1=mask_t[:, t_:t_ + 1], scalar2=None,
                                       op0=ALU.mult)
            VF(v, inst)
            for t_ in range(16):
                inst = v.tensor_copy(out=psb[:, t_, :], in_=psbf[:, t_, :])
            inst.then_inc(s_m1, 1)

        @blk.tensor
        def _(t):
            t.wait_ge(s_m1, 5)
            for t_ in range(16):
                for r in range(3):
                    ix = t_ * 3 + r
                    if ix >= 1:
                        t.wait_ge(s_pa, 2 + ix)
                    t.transpose(out=ptp[:], in_=psb[:, t_, r * 128:(r + 1) * 128],
                                identity=idn[:]).then_inc(s_pm, 1)

        @blk.scalar
        def _(s):
            for t_ in range(16):
                for r in range(3):
                    ix = t_ * 3 + r
                    s.wait_ge(s_pm, 2 + ix)
                    s.copy(out=late['pt_sb'][:, r, t_, :], in_=ptp[:]).then_inc(s_pa, 1)

        @blk.sync
        def _(sync):
            sync.wait_ge(s_ks, 16 * 320)
            for t_ in range(8):
                sync.dma_start(out=xloc[:, t_, :],
                               in_=xnd[t_ * 128:(t_ + 1) * 128, :]).then_inc(s_gl, 16)

        @blk.scalar
        def _(s):
            s.wait_ge(s_ks, 16 * 320)
            for t_ in range(8, 16):
                s.dma_start(out=xloc[:, t_, :],
                            in_=xnd[t_ * 128:(t_ + 1) * 128, :]).then_inc(s_gl, 16)

        @blk.tensor
        def _(t):
            t.wait_ge(s_gl, 256)
            for c in range(16):
                gb = pA[c % 2][:, 0:NSLOT]
                if c >= 2:
                    t.wait_ge(s_pa, 49 + c)
                for t_ in range(16):
                    mm = t.matmul(out=gb, lhsT=xloc[:, t_, c * 128:(c + 1) * 128],
                                  rhs=psb[:, t_, :], start=(t_ == 0), stop=(t_ == 15))
                mm.then_inc(s_pm, 1)
            for r in range(3):
                if r >= 1:
                    t.wait_ge(s_m1, 5 + r)
                gb = pA[2][:, 0:1]
                for t_ in range(16):
                    mm = t.matmul(out=gb, lhsT=psbf[:, t_, r * 128:(r + 1) * 128],
                                  rhs=gw_t[:, t_:t_ + 1], start=(t_ == 0), stop=(t_ == 15))
                mm.then_inc(s_pm, 1)

        @blk.scalar
        def _(s):
            for c in range(16):
                s.wait_ge(s_pm, 50 + c)
                s.copy(out=late['xgt'][:, c, :], in_=pA[c % 2][:, 0:NSLOT]).then_inc(s_pa, 1)

        @blk.vector
        def _(v):
            for r in range(3):
                v.wait_ge(s_pm, 66 + r)
                v.tensor_copy(out=late['gwcol'][:, r:r + 1], in_=pA[2][:, 0:1]).then_inc(s_m1, 1)

    def ph_p4(st):
        ht = st.enter_context(nc.sbuf_tensor("ht", [128, 64, NSLOT], BF16))
        b2_bc = st.enter_context(nc.sbuf_tensor("b2_bc", [128, D], F32))
        u_f = [st.enter_context(nc.sbuf_tensor(f"u{i}", [128, NSLOT], F32)) for i in range(2)]
        g2 = [st.enter_context(nc.sbuf_tensor(f"g2_{i}", [128, NSLOT], F32)) for i in range(2)]
        g3 = [st.enter_context(nc.sbuf_tensor(f"g3_{i}", [128, NSLOT], F32)) for i in range(2)]
        tn = [st.enter_context(nc.sbuf_tensor(f"tn{i}", [128, NSLOT], F32)) for i in range(2)]

        with contextlib.ExitStack() as st2:
            w1s = [st2.enter_context(nc.sbuf_tensor(f"w1s{i}", [128, 16, 512], BF16)) for i in range(2)]

            @blk.gpsimd
            def _(g):
                g.wait_ge(s_pm, 68)
                g.dma_start(out=b2_bc[:], in_=b2.partition_broadcast(128)).then_inc(s_bb, 16)
                for j in range(16):
                    if j >= 2:
                        g.wait_ge(s_f1, 4 * (j - 1))
                    g.dma_start(out=w1s[j % 2][:],
                                in_=w1[:, j * 512:(j + 1) * 512]
                                .rearrange("(c p) d -> p c d", p=128)).then_inc(s_wl, 16)

            @blk.tensor
            def _(t):
                t.wait_ge(s_pa, 66)
                for j in range(16):
                    t.wait_ge(s_wl, 80 + 16 * (j + 1))
                    for m2 in range(4):
                        ix = j * 4 + m2
                        ub = pA[ix % 3][:, 0:NSLOT]
                        if ix >= 3:
                            t.wait_ge(s_f2, 2 * (ix - 3) + 1)
                        for c in range(16):
                            mm = t.matmul(out=ub, lhsT=w1s[j % 2][:, c, m2 * 128:(m2 + 1) * 128],
                                          rhs=late['xgt'][:, c, :], start=(c == 0), stop=(c == 15))
                        mm.then_inc(s_f1, 1)

            @blk.scalar
            def _(s):
                for ix in range(64):
                    s.wait_ge(s_f1, ix + 1)
                    if ix >= 2:
                        s.wait_ge(s_f3, 2 * ix - 2)
                    s.activation(out=u_f[ix % 2][:], in_=pA[ix % 3][:, 0:NSLOT],
                                 func=AF.Identity, bias=b1c[:, ix:ix + 1],
                                 scale=1.0).then_inc(s_f2, 1)
                    s.wait_ge(s_f3, 2 * ix + 1)
                    s.activation(out=tn[ix % 2][:], in_=tn[ix % 2][:], func=AF.Tanh,
                                 scale=0.7978845608028654).then_inc(s_f2, 1)

            @blk.vector
            def _(v):
                for ix in range(64):
                    v.wait_ge(s_f2, 2 * ix + 1)
                    u = u_f[ix % 2]
                    VF(v, v.tensor_tensor(out=g2[ix % 2][:], in0=u[:], in1=u[:], op=ALU.mult))
                    VF(v, v.tensor_tensor(out=g3[ix % 2][:], in0=g2[ix % 2][:], in1=u[:], op=ALU.mult))
                    v.scalar_tensor_tensor(out=tn[ix % 2][:], in0=g3[ix % 2][:],
                                           scalar=0.044715, in1=u[:],
                                           op0=ALU.mult, op1=ALU.add).then_inc(s_f3, 1)
                    v.wait_ge(s_f2, 2 * ix + 2)
                    VF(v, v.tensor_scalar(out=g2[ix % 2][:], in0=tn[ix % 2][:], scalar1=0.5,
                                          scalar2=0.5, op0=ALU.mult, op1=ALU.add))
                    v.tensor_tensor(out=ht[:, ix, :], in0=u[:], in1=g2[ix % 2][:],
                                    op=ALU.mult).then_inc(s_f3, 1)

        with contextlib.ExitStack() as st2:
            w2s = [st2.enter_context(nc.sbuf_tensor(f"w2s{i}", [128, 64, 256], BF16)) for i in range(2)]

            @blk.gpsimd
            def _(g):
                g.wait_ge(s_f1, 64)
                for j in range(8):
                    if j >= 2:
                        g.wait_ge(s_f4, 3 * (j - 1))
                    g.dma_start(out=w2s[j % 2][:],
                                in_=w2[:, j * 256:(j + 1) * 256]
                                .rearrange("(c p) d -> p c d", p=128)).then_inc(s_wl, 16)

            @blk.tensor
            def _(t):
                t.wait_ge(s_f3, 128)
                for j in range(8):
                    t.wait_ge(s_wl, 336 + 16 * (j + 1))
                    for r in range(3):
                        ix = j * 3 + r
                        vb = pA[ix % 3][:, 512:768]
                        if ix >= 3:
                            t.wait_ge(s_f5, ix - 2)
                        for c in range(64):
                            mm = t.matmul(out=vb, lhsT=ht[:, c, r * 128:(r + 1) * 128],
                                          rhs=w2s[j % 2][:, c, :], start=(c == 0),
                                          stop=(c == 63))
                        mm.then_inc(s_f4, 1)

            @blk.vector
            def _(v):
                v.wait_ge(s_m1, 8)
                v.wait_ge(s_bb, 16)
                for j in range(8):
                    for r in range(3):
                        ix = j * 3 + r
                        sl2 = slice(j * 256, (j + 1) * 256)
                        v.wait_ge(s_f4, ix + 1)
                        VF(v, v.tensor_tensor(out=tn[0][:, 0:256], in0=pA[ix % 3][:, 512:768],
                                              in1=b2_bc[:, sl2], op=ALU.add))
                        VF(v, v.tensor_scalar(out=tn[1][:, 0:256], in0=tn[0][:, 0:256],
                                              scalar1=late['gwcol'][:, r:r + 1],
                                              scalar2=None, op0=ALU.mult))
                        VF(v, v.tensor_copy(out=late['valsg'][:, r, sl2], in_=tn[1][:, 0:256]))
                        VF(v, v.tensor_copy(out=g2[0][:, 0:256], in_=late['valsg'][:, r, sl2]))
                        VF(v, v.tensor_tensor(out=g3[0][:, 0:256], in0=tn[1][:, 0:256],
                                              in1=g2[0][:, 0:256], op=ALU.subtract))
                        v.tensor_copy(out=late['valsl'][:, r, sl2],
                                      in_=g3[0][:, 0:256]).then_inc(s_f5, 1)

    def ph_p3(st):
        esb = st.enter_context(nc.sbuf_tensor("esb", [128, 16, 1024], BF16))
        qt = [st.enter_context(nc.sbuf_tensor(f"qt{i}", [128, SL], BF16)) for i in range(2)]
        kt = [st.enter_context(nc.sbuf_tensor(f"kt{i}", [128, S], BF16)) for i in range(2)]
        vt = [st.enter_context(nc.sbuf_tensor(f"vt{i}", [128, 32, 128], BF16)) for i in range(2)]
        rd_sb = st.enter_context(nc.sbuf_tensor("rd_sb", [1, 1024], F32))
        ftmp = st.enter_context(nc.sbuf_tensor("ftmp", [128, 512], BF16))
        den = st.enter_context(nc.psum_tensor("den", [128, 512], F32))

        @blk.sync
        def _(sync):
            sync.wait_ge(s_ks, 16 * 320)
            sync.wait_ge(s_f4, 24)
            for p in range(NPAIR):
                if p >= 2:
                    sync.wait_ge(s_av, (p - 1) * 16)
                sync.dma_start(out=qt[p % 2][:],
                               in_=qtd[p * 128:(p + 1) * 128, :]).then_inc(s_pl, 16)
                sync.dma_start(out=kt[p % 2][:],
                               in_=ktd[p * 128:(p + 1) * 128, :]).then_inc(s_pl, 16)
                for c in range(32):
                    sync.dma_start(out=vt[p % 2][:, c, :],
                                   in_=vtd[p * 128:(p + 1) * 128, c * 128:(c + 1) * 128],
                                   transpose=True).then_inc(s_pl, 16)

        def emit_ftmp(s, hb):
            s.wait_ge(s_av, (hb + 1) * 4)
            if hb > 0:
                s.wait_ge(s_fn, hb)
            s.copy(out=ftmp[:], in_=pB0[:]).then_inc(s_fe, 1)

        def emit_exps(s, p):
            for b in range(4):
                hb = p * 4 + b
                for q in range(4):
                    eg = hb * 4 + q
                    for j in range(8):
                        nsc = eg * 8 + j + 1
                        s.wait_ge(s_sc, nsc)
                        if j == 0 and eg > 1:
                            s.wait_ge(s_av, eg - 1)
                        s.activation(out=esb[:, (eg % 2) * 8 + j, :],
                                     in_=pA[(nsc - 1) % 2][:],
                                     func=AF.Exp, scale=SCALE).then_inc(s_ex, 1)
                        if q == 0 and j == 6 and hb > 0:
                            emit_ftmp(s, hb - 1)

        @blk.scalar
        def _(s):
            for p in range(NPAIR):
                emit_exps(s, p)
            emit_ftmp(s, 63)

        def ep_bcast(t, hb):
            t.wait_ge(s_rc, hb + 1)
            if hb >= 1:
                t.wait_ge(s_fn, hb)
            t.matmul(out=pA[2][0:64, 512:1024], lhsT=ones_rf[0:1, 0:64],
                     rhs=rd_sb[0:1, 0:512], start=True, stop=True,
                     tile_position=(0, 0), skip_group_check=True)
            t.matmul(out=pA[2][64:128, 512:1024], lhsT=ones_rf[0:1, 0:64],
                     rhs=rd_sb[0:1, 512:1024], start=True, stop=True,
                     tile_position=(0, 64),
                     skip_group_check=True).then_inc(s_bcm, 1)

        def phase2_chunk(t, gm, j):
            p, q = gm // 16, gm % 4
            dn = den if (gm // 4) % 2 == 0 else pA[2]
            c = q * 8 + j
            es = (gm % 2) * 8 + j
            first = (q == 0 and j == 0)
            last = (q == 3 and j == 7)
            t.matmul(out=pB0[0:64, :], lhsT=vt[p % 2][:, c, 0:64],
                     rhs=esb[:, es, 0:512], start=first, stop=last,
                     tile_position=(0, 0), skip_group_check=True)
            t.matmul(out=pB0[64:128, :], lhsT=vt[p % 2][:, c, 64:128],
                     rhs=esb[:, es, 512:1024], start=first, stop=last,
                     tile_position=(0, 64), skip_group_check=True)
            t.matmul(out=dn[0:1, 0:512], lhsT=ones_col[:],
                     rhs=esb[:, es, 0:512], start=first, stop=last,
                     tile_position=(0, 0), skip_group_check=True)
            mm = t.matmul(out=dn[64:65, 0:512], lhsT=ones_col[:],
                          rhs=esb[:, es, 512:1024], start=first, stop=last,
                          tile_position=(0, 64), skip_group_check=True)
            if j == 7:
                mm.then_inc(s_av, 1)

        @blk.tensor
        def _(t):
            for g in range(64 * 4):
                p, b, q = g // 16, (g // 4) % 4, g % 4
                hb = g // 4
                if g % 16 == 0:
                    t.wait_ge(s_pl, 34 * 16 * (p + 1))
                for j in range(8):
                    c = q * 8 + j
                    nsc = g * 8 + j + 1
                    t.matmul(out=den[32:33, 0:64], lhsT=ones_col[:],
                             rhs=kt[p % 2][:, 0:64], start=True,
                             stop=True, skip_group_check=True)
                    if q == 2 and j == 4 and hb > 0:
                        ep_bcast(t, hb - 1)
                    if nsc > 2:
                        t.wait_ge(s_ex, nsc - 2)
                    t.matmul(out=pA[(nsc - 1) % 2][:, 0:512],
                             lhsT=kt[p % 2][0:64, c * 128:(c + 1) * 128],
                             rhs=qt[p % 2][0:64, b * 512:(b + 1) * 512],
                             start=True, stop=True, tile_position=(0, 0))
                    t.matmul(out=pA[(nsc - 1) % 2][:, 512:1024],
                             lhsT=kt[p % 2][64:128, c * 128:(c + 1) * 128],
                             rhs=qt[p % 2][64:128, b * 512:(b + 1) * 512],
                             start=True, stop=True,
                             tile_position=(64, 0)).then_inc(s_sc, 1)
                    if g >= 1:
                        if (g - 1) % 4 == 0 and j == 0:
                            hbm = (g - 1) // 4
                            t.wait_ge(s_fe, hbm)
                            if hbm >= 2:
                                t.wait_ge(s_rc, hbm - 1)
                        phase2_chunk(t, g - 1, j)
            t.wait_ge(s_ex, 64 * 4 * 8)
            for j in range(8):
                phase2_chunk(t, 64 * 4 - 1, j)
            ep_bcast(t, 63)

        @blk.vector
        def _(v):
            for p in range(NPAIR):
                for b in range(4):
                    hb = p * 4 + b
                    v.wait_ge(s_av, (hb + 1) * 4)
                    if hb >= 1:
                        v.wait_ge(s_bcm, hb)
                    dn = den if hb % 2 == 0 else pA[2]
                    v.reciprocal(out=rd_sb[0:1, 0:512], in_=dn[0:1, 0:512])
                    v.reciprocal(out=rd_sb[0:1, 512:1024],
                                 in_=dn[64:65, 0:512]).then_inc(s_rc, 1)
                    v.wait_ge(s_bcm, hb + 1)
                    v.wait_ge(s_fe, hb + 1)
                    v.tensor_tensor(out=late['featsT'][:, p, b * 512:(b + 1) * 512],
                                    in0=ftmp[:], in1=pA[2][:, 512:1024],
                                    op=ALU.mult).then_inc(s_fn, 1)

    def ph_p5(st):
        wos = late["wos"]
        pt5 = st.enter_context(nc.sbuf_tensor("pt5", [128, 3, 16, 128], BF16))
        bo_bc = st.enter_context(nc.sbuf_tensor("bo_bc", [128, D], F32))
        xt2 = [st.enter_context(nc.sbuf_tensor(f"xt2_{i}", [128, 1024], F32)) for i in range(2)]
        atmp = st.enter_context(nc.sbuf_tensor("atmp", [128, 512], F32))
        stg = [st.enter_context(nc.sbuf_tensor(f"stg{i}", [128, 1024], F32)) for i in range(2)]

        for half in range(2):
            @blk.gpsimd
            def _(g, half=half):
                if half == 1:
                    g.wait_ge(s_om, 32)
                if half == 0:
                    g.wait_ge(s_f4, 24)
                g.dma_start(out=wos[:],
                            in_=wo[:, half * 1024:(half + 1) * 1024]
                            .rearrange("(c p) d -> p c d", p=128)).then_inc(s_wl, 16)
                if half == 0:
                    g.wait_ge(s_bcm, 64)
                    g.dma_start(out=bo_bc[:], in_=bo.partition_broadcast(128)).then_inc(s_bb, 16)

            @blk.sync
            def _(sync, half=half):
                if half == 0:
                    sync.wait_ge(s_bcm, 64)
                    sync.wait_ge(s_fn, 64)
                    sync.wait_ge(s_ptd, 16)
                    sync.dma_start(out=pt5[:],
                                   in_=ptd.rearrange("p (a b c) -> p a b c",
                                                     a=3, b=16)).then_inc(s_ptd, 16)
                for m in range(16):
                    ix = half * 16 + m
                    if ix >= 2:
                        sync.wait_ge(s_oa, 2 * (ix - 1))
                    sync.dma_start(out=xt2[m % 2][:],
                                   in_=x[m * 128:(m + 1) * 128,
                                         half * 1024:(half + 1) * 1024]).then_inc(s_x2, 16)
                    if m >= 2:
                        sync.wait_ge(s_oa, 2 * (ix - 1))
                        sync.dma_start(out=out[(m - 2) * 128:(m - 1) * 128,
                                               half * 1024:(half + 1) * 1024],
                                       in_=stg[m % 2][:]).then_inc(s_os, 16)
                for m in range(14, 16):
                    ix = half * 16 + m
                    sync.wait_ge(s_oa, 2 * (ix + 1))
                    sync.dma_start(out=out[m * 128:(m + 1) * 128,
                                           half * 1024:(half + 1) * 1024],
                                   in_=stg[m % 2][:]).then_inc(s_os, 16)

            @blk.tensor
            def _(t, half=half):
                t.wait_ge(s_wl, 464 + 16 * (half + 1))
                if half == 0:
                    t.wait_ge(s_fn, 64)
                    t.wait_ge(s_f5, 24)
                    t.wait_ge(s_ptd, 32)
                for m in range(16):
                    for n2 in range(2):
                        ix = (half * 16 + m) * 2 + n2
                        ob = pA[ix % 3][:, 0:512]
                        if ix >= 3:
                            t.wait_ge(s_oa, ix - 2)
                        for pr in range(16):
                            t.matmul(out=ob, lhsT=late['featsT'][:, pr, m * 128:(m + 1) * 128],
                                     rhs=wos[:, pr, n2 * 512:(n2 + 1) * 512],
                                     start=(pr == 0), stop=False,
                                     skip_group_check=True)
                        for r in range(3):
                            t.matmul(out=ob, lhsT=pt5[:, r, m, :],
                                     rhs=late['valsg'][:, r, half * 1024 + n2 * 512:
                                               half * 1024 + (n2 + 1) * 512],
                                     start=False, stop=False,
                                     skip_group_check=True)
                        for r in range(3):
                            mm = t.matmul(out=ob, lhsT=pt5[:, r, m, :],
                                          rhs=late['valsl'][:, r, half * 1024 + n2 * 512:
                                                    half * 1024 + (n2 + 1) * 512],
                                          start=False, stop=(r == 2),
                                          skip_group_check=True)
                        mm.then_inc(s_om, 1)

            @blk.vector
            def _(v, half=half):
                if half == 0:
                    v.wait_ge(s_bb, 32)
                for m in range(16):
                    for n2 in range(2):
                        ix = (half * 16 + m) * 2 + n2
                        v.wait_ge(s_om, ix + 1)
                        v.wait_ge(s_x2, 16 * (half * 16 + m + 1))
                        if half * 16 + m >= 2:
                            v.wait_ge(s_os, 16 * (half * 16 + m - 1))
                        sl = slice(n2 * 512, (n2 + 1) * 512)
                        fsl = slice(half * 1024 + n2 * 512, half * 1024 + (n2 + 1) * 512)
                        v.tensor_scalar(out=atmp[:], in0=xt2[m % 2][:, sl],
                                        scalar1=a1_t[:, m:m + 1],
                                        scalar2=b1p_t[:, m:m + 1],
                                        op0=ALU.mult, op1=ALU.add)
                        VF(v, v.tensor_tensor(out=stg[m % 2][:, sl],
                                              in0=pA[ix % 3][:, 0:512],
                                              in1=bo_bc[:, fsl], op=ALU.add))
                        v.tensor_tensor(out=stg[m % 2][:, sl], in0=stg[m % 2][:, sl],
                                        in1=atmp[:], op=ALU.add).then_inc(s_oa, 1)


        @blk.sync
        def _(sync):
            sync.wait_ge(s_os, 16 * 32)

    with contextlib.ExitStack() as st, nc.named_scope("Pc"):
        ph_consts(st)
    with contextlib.ExitStack() as stw:
        stq = contextlib.ExitStack()
        early = dict(
            xnt=[stw.enter_context(nc.sbuf_tensor(f"xnt{i}", [128, 16, 512], BF16)) for i in range(2)],
            kst=[stw.enter_context(nc.sbuf_tensor(f"kst{i}", [128, 512], BF16)) for i in range(16)],
        )
        for nm, shp, dt in [("iota_i", [128, NSLOT], I32), ("iota384", [128, NSLOT], F32),
                            ("iota1r", [1, 16], F32), ("mrow", [1, 16], F32),
                            ("brow", [1, 18], F32), ("lor", [1, 1], F32),
                            ("hir", [1, 1], F32), ("hrow", [1, 1], F32),
                            ("t16a", [1, 16], F32), ("t16b", [1, 16], F32),
                            ("s1r", [1, 1], F32), ("s2r", [1, 1], F32),
                            ("q1r", [1, 1], F32), ("q2r", [1, 1], F32),
                            ("mbs", [128, 16], F32), ("cscr", [128, 16, 32], F32),
                            ("p16", [128, 16], F32), ("p16b", [128, 16], BF16)]:
            early[nm] = stw.enter_context(nc.sbuf_tensor(nm, shp, dt))
        early["wqs"] = stq.enter_context(nc.sbuf_tensor("wqs", [128, 16, 2048], BF16))
        early["stq"] = stq
        if upto >= 0:
            with contextlib.ExitStack() as st, nc.named_scope("P0"):
                ph_p0(st)
        if upto >= 1:
            with contextlib.ExitStack() as st, nc.named_scope("P2"):
                ph_p2(st, early)
    late["gwcol"] = A("gwcol", [128, 3], F32)
    late["valsg"] = A("valsg", [128, 3, D], BF16)
    late["valsl"] = A("valsl", [128, 3, D], BF16)
    stx = contextlib.ExitStack()
    late["xgt"] = stx.enter_context(nc.sbuf_tensor("xgt", [128, 16, NSLOT], BF16))
    stp = contextlib.ExitStack()
    late["pt_sb"] = stp.enter_context(nc.sbuf_tensor("pt_sb", [128, 3, 16, 128], BF16))
    if upto >= 2:
        with contextlib.ExitStack() as st, nc.named_scope("P2b"):
            ph_p2b(st, early)

        @blk.sync
        def _(sync):
            sync.wait_ge(s_pa, 50)
            sync.dma_start(out=ptd,
                           in_=late["pt_sb"].ap().rearrange("p a b c -> p (a b c)")
                           ).then_inc(s_ptd, 16)
    stp.close()
    if upto >= 3:
        with contextlib.ExitStack() as st, nc.named_scope("P4"):
            ph_p4(st)
        if dbg:
            xgtd = nc.dram_tensor("xgtd", [128, 16, NSLOT], BF16, kind="Internal").ap()

            @blk.sync
            def _(sync):
                sync.wait_ge(s_pa, 66)
                sync.dma_start(out=xgtd, in_=late["xgt"][:])
    stx.close()
    late["featsT"] = A("featsT", [128, 16, SL], BF16)
    late["wos"] = A("wos", [128, 16, 1024], BF16)
    if upto >= 4:
        with contextlib.ExitStack() as st, nc.named_scope("P3"):
            ph_p3(st)
    if upto >= 5:
        with contextlib.ExitStack() as st, nc.named_scope("P5"):
            ph_p5(st)

    if dbg:
        dоко = None
        dbg_done = SEM("s_dbg")
        dbgs = []

        def dump_dram(name, src_ap, shape, dt):
            o = nc.dram_tensor(name, shape, dt, kind="ExternalOutput").ap()
            dbgs.append((o, src_ap))

        @blk.sync
        def _(sync):
            if upto >= 0:
                sync.wait_ge(s_xnst, 16 * 32)
            if upto >= 1:
                sync.wait_ge(s_ks, 16 * 320)
            if upto >= 2:
                sync.wait_ge(s_m1, 8)
                sync.wait_ge(s_pa, 66)
            if upto >= 3:
                sync.wait_ge(s_f5, 24)
            if upto >= 4:
                sync.wait_ge(s_fn, 64)
            n = 0
            def dump(name, src_ap, shape, dt):
                nonlocal n
                o = nc.dram_tensor(name, shape, dt, kind="ExternalOutput").ap()
                sync.dma_start(out=o, in_=src_ap).then_inc(dbg_done, 16)
                n += 1
            if upto >= 0:
                dump("d_xn", xnd, [S, D], BF16)
                dump("d_w", w_all[:], [128, 32], F32)
                dump("d_mean", mean_all[:], [128, 32], F32)
                dump("d_rstd", rstd_all[:], [128, 32], F32)
            if upto >= 1:
                dump("d_qt", qtd, [D, SL], BF16)
                dump("d_kt", ktd, [D, S], BF16)
                dump("d_vt", vtd, [D, S], BF16)
            if upto >= 2:
                dump("d_hi", hi_c[:], [128, 1], F32)
                dump("d_mask", mask_t[:], [128, 16], F32)
                dump("d_rank", rank_t[:], [128, 16], F32)
                dump("d_xgt", xgtd, [128, 16, NSLOT], BF16)
                dump("d_gwc", late["gwcol"][:], [128, 3], F32)
                dump("d_pt", ptd, [128, 3, 16, 128], BF16)
            if upto >= 3:
                dump("d_vals", late["valsg"][:], [128, 3, D], BF16)
            if upto >= 4:
                dump("d_ft", late["featsT"][:], [128, 16, SL], BF16)
            sync.wait_ge(dbg_done, 16 * n)

    es.close()
    return nc


_NC_CACHE = {}


def _get_nc():
    if "nc" not in _NC_CACHE:
        _NC_CACHE["nc"] = build_nc()
    return _NC_CACHE["nc"]


def make_in_maps(L0_attn_x, L0_attn_Wq, L0_attn_bq, L0_attn_Wk, L0_attn_bk,
                 L0_attn_Wv, L0_attn_bv, L0_attn_Wo, L0_attn_bo,
                 L0_ffn_W_r, L0_ffn_b_r, L0_ffn_W1, L0_ffn_b1, L0_ffn_W2,
                 L0_ffn_b2, L0_ffn_position_ids=None, L0_ffn_cache_position=None,
                 **_unused):
    x = np.ascontiguousarray(np.asarray(L0_attn_x, dtype=np.float32))
    consts = dict(
        wq=np.ascontiguousarray(np.asarray(L0_attn_Wq, np.float32)),
        wk=np.ascontiguousarray(np.asarray(L0_attn_Wk, np.float32)),
        wv=np.ascontiguousarray(np.asarray(L0_attn_Wv, np.float32)),
        wo=np.ascontiguousarray(np.asarray(L0_attn_Wo, np.float32)),
        bq=np.ascontiguousarray(np.asarray(L0_attn_bq, np.float32).reshape(D)),
        bk=np.ascontiguousarray(np.asarray(L0_attn_bk, np.float32).reshape(D)),
        bv=np.ascontiguousarray(np.asarray(L0_attn_bv, np.float32).reshape(D)),
        bo=np.ascontiguousarray(np.asarray(L0_attn_bo, np.float32).reshape(D)),
        wr=np.ascontiguousarray(np.asarray(L0_ffn_W_r, np.float32).reshape(D, 1)),
        br=np.ascontiguousarray(np.asarray(L0_ffn_b_r, np.float32).reshape(1)),
        w1=np.ascontiguousarray(np.asarray(L0_ffn_W1, np.float32)),
        b1=np.ascontiguousarray(np.asarray(L0_ffn_b1, np.float32).reshape(DFF)),
        w2=np.ascontiguousarray(np.asarray(L0_ffn_W2, np.float32)),
        b2=np.ascontiguousarray(np.asarray(L0_ffn_b2, np.float32).reshape(D)),
    )
    in_maps = []
    for core in range(8):
        b, h = core // 2, core % 2
        m = dict(consts)
        m["x"] = np.ascontiguousarray(np.roll(x[b], -h * SL, axis=0))
        in_maps.append(m)
    return in_maps


def kernel(**inputs):
    in_maps = make_in_maps(**inputs)
    nc = _get_nc()
    res = run_bass_kernel_spmd(nc, in_maps, core_ids=list(range(8)))
    outp = np.empty((4, 4096, 2048), dtype=np.float32)
    for core in range(8):
        b, h = core // 2, core % 2
        outp[b, h * SL:(h + 1) * SL] = res.results[core]["out"]
    return outp


# version 18
# speedup vs baseline: 1.0429x; 1.0429x over previous
import math
import contextlib
import numpy as np
import ml_dtypes
import concourse.bass as bass
import concourse.mybir as mybir
from concourse.bass_utils import run_bass_kernel_spmd

F32 = mybir.dt.float32
BF16 = mybir.dt.bfloat16
I32 = mybir.dt.int32
AF = mybir.ActivationFunctionType
ALU = mybir.AluOpType
AX = mybir.AxisListType

S = 4096
SL = 2048
D = 2048
DFF = 8192
NPAIR = 16
NSLOT = 384
SCALE = 1.0 / math.sqrt(32.0)
NBIS = 10
EPS = 1e-5


def build_nc(upto=99, dbg=False):
    nc = bass.Bass("TRN2", target_bir_lowering=False,
                   detect_race_conditions=False)

    x = nc.dram_tensor("x", [S, D], F32, kind="ExternalInput").ap()
    wq = nc.dram_tensor("wq", [D, D], F32, kind="ExternalInput").ap()
    wk = nc.dram_tensor("wk", [D, D], F32, kind="ExternalInput").ap()
    wv = nc.dram_tensor("wv", [D, D], F32, kind="ExternalInput").ap()
    wo = nc.dram_tensor("wo", [D, D], F32, kind="ExternalInput").ap()
    bq = nc.dram_tensor("bq", [D], F32, kind="ExternalInput").ap()
    bk = nc.dram_tensor("bk", [D], F32, kind="ExternalInput").ap()
    bv = nc.dram_tensor("bv", [D], F32, kind="ExternalInput").ap()
    bo = nc.dram_tensor("bo", [D], F32, kind="ExternalInput").ap()
    wr = nc.dram_tensor("wr", [D, 1], F32, kind="ExternalInput").ap()
    br = nc.dram_tensor("br", [1], F32, kind="ExternalInput").ap()
    w1 = nc.dram_tensor("w1", [D, DFF], F32, kind="ExternalInput").ap()
    b1 = nc.dram_tensor("b1", [DFF], F32, kind="ExternalInput").ap()
    w2 = nc.dram_tensor("w2", [DFF, D], F32, kind="ExternalInput").ap()
    b2 = nc.dram_tensor("b2", [D], F32, kind="ExternalInput").ap()
    out = nc.dram_tensor("out", [SL, D], F32, kind="ExternalOutput").ap()

    xnd = nc.dram_tensor("xnd", [S, D], BF16, kind="Internal").ap()
    qtd = nc.dram_tensor("qtd", [D, SL], BF16, kind="Internal").ap()
    ktd = nc.dram_tensor("ktd", [D, S], BF16, kind="Internal").ap()
    vtd = nc.dram_tensor("vtd", [D, S], BF16, kind="Internal").ap()
    ptd = nc.dram_tensor("ptd", [128, 3 * 16 * 128], BF16, kind="Internal").ap()

    A = nc.alloc_sbuf_tensor
    SEM = nc.alloc_semaphore

    idn = A("idn", [128, 128], BF16)
    idn_f = A("idn_f", [128, 128], F32)
    ltri = A("ltri", [128, 128], BF16)
    ones_col = A("ones_col", [128, 1], BF16)
    ones_rf = A("ones_rf", [1, 128], F32)

    mean_all = A("mean_all", [128, 32], F32)
    rstd_all = A("rstd_all", [128, 32], F32)
    nb_all = A("nb_all", [128, 32], F32)
    w_all = A("w_all", [128, 32], F32)
    mv = [A(f"mv{i}", [128, 2], F32) for i in range(2)]
    std_c = [A(f"std{i}", [128, 1], F32) for i in range(2)]

    br_col = A("br_col", [128, 1], F32)
    bqc = A("bqc", [128, 16], F32)
    bkc = A("bkc", [128, 16], F32)
    bvc = A("bvc", [128, 16], F32)
    b1c = A("b1c", [128, 64], F32)

    lo_c = A("lo_c", [128, 1], F32)
    hi_c = A("hi_c", [128, 1], F32)
    mid_c = A("mid_c", [128, 1], F32)
    cnd_c = A("cnd_c", [128, 1], F32)
    d1_c = A("d1_c", [128, 1], F32)
    d2_c = A("d2_c", [128, 1], F32)
    cmp_t = A("cmp_t", [128, 32], F32)
    part_c = A("part_c", [128, 1], F32)
    part_b = A("part_b", [128, 1], BF16)
    cnt_sb = A("cnt_sb", [1, 1], F32)

    mask_t = A("mask_t", [128, 16], F32)
    notm_t = A("notm_t", [128, 16], F32)
    a1_t = A("a1_t", [128, 16], F32)
    b1p_t = A("b1p_t", [128, 16], F32)
    gw_t = A("gw_t", [128, 16], F32)
    incl_t = A("incl_t", [128, 16], F32)
    rank_t = A("rank_t", [128, 16], F32)
    rtot_b = A("rtot_b", [128, 1], BF16)
    offs_c = A("offs_c", [128, 1], F32)
    zeros16 = A("zeros16", [128, 16], F32)
    eps_col = A("eps_col", [128, 1], F32)

    late = {}

    pA = [nc.alloc_psum_tensor(f"pA{i}", [128, 1024], F32) for i in range(3)]
    pB0 = nc.alloc_psum_tensor("pB0", [128, 512], F32)

    s_gio = SEM("s_gio"); s_cst = SEM("s_cst"); s_cb = SEM("s_cb")
    s_ct = SEM("s_ct"); s_x = SEM("s_x"); s_d1 = SEM("s_d1")
    s_a1 = SEM("s_a1"); s_d2 = SEM("s_d2"); s_a2 = SEM("s_a2")
    s_d3 = SEM("s_d3"); s_xnst = SEM("s_xnst"); s_wl = SEM("s_wl")
    s_tr = SEM("s_tr"); s_qm = SEM("s_qm"); s_qe = SEM("s_qe")
    s_ks = SEM("s_ks"); s_xf = SEM("s_xf")
    s_bm = SEM("s_bm"); s_bd = SEM("s_bd"); s_bp = SEM("s_bp")
    s_ba = SEM("s_ba"); s_bc = SEM("s_bc")
    s_m1 = SEM("s_m1"); s_pm = SEM("s_pm"); s_pa = SEM("s_pa"); s_ptd = SEM("s_ptd")
    s_gl = SEM("s_gl")
    s_f1 = SEM("s_f1"); s_f2 = SEM("s_f2"); s_f3 = SEM("s_f3")
    s_f4 = SEM("s_f4"); s_f5 = SEM("s_f5")
    s_pl = SEM("s_pl"); s_pl2 = SEM("s_pl2"); s_sc = SEM("s_sc")
    s_ex = SEM("s_ex"); s_av = SEM("s_av"); s_rc = SEM("s_rc")
    s_bcm = SEM("s_bcm"); s_fe = SEM("s_fe"); s_fn = SEM("s_fn")
    s_x2 = SEM("s_x2"); s_om = SEM("s_om"); s_oa = SEM("s_oa")
    s_bb = SEM("s_bb"); s_fl = SEM("s_fl"); s_fla = SEM("s_fla")
    flc = [0]
    flca = [0]

    def VF(v, inst):
        inst.then_inc(s_fl, 1)
        flc[0] += 1
        v.wait_ge(s_fl, flc[0])

    def SF(s, inst):
        inst.then_inc(s_fla, 1)
        flca[0] += 1
        s.wait_ge(s_fla, flca[0])
    s_os = SEM("s_os")

    es = contextlib.ExitStack()
    blk = es.enter_context(nc.Block())

    def ph_consts(st):
        scr_i = st.enter_context(nc.sbuf_tensor("scr_i", [128, NSLOT], I32))
        scr_f = st.enter_context(nc.sbuf_tensor("scr_f", [128, 128], F32))
        rowv = st.enter_context(nc.sbuf_tensor("rowv", [128, 1], F32))
        btile = st.enter_context(nc.sbuf_tensor("btile", [128, 128], F32))
        @blk.gpsimd
        def _(g):
            g.iota(scr_i[:, 0:128], pattern=[[1, 128]], base=0,
                   channel_multiplier=0).then_inc(s_gio, 1)
            g.iota(scr_i[:, 128:129], pattern=[[0, 1]], base=0,
                   channel_multiplier=1).then_inc(s_gio, 1)

        @blk.sync
        def _(sync):
            sync.dma_start(out=br_col[:], in_=br.partition_broadcast(128)).then_inc(s_cst, 16)
            sync.dma_start(out=btile[0:16, :], in_=bq.rearrange("(c p) -> c p", p=128)).then_inc(s_cst, 16)
            sync.dma_start(out=btile[16:32, :], in_=bk.rearrange("(c p) -> c p", p=128)).then_inc(s_cst, 16)
            sync.dma_start(out=btile[32:48, :], in_=bv.rearrange("(c p) -> c p", p=128)).then_inc(s_cst, 16)

        @blk.vector
        def _(v):
            v.wait_ge(s_gio, 2)
            v.tensor_copy(out=rowv[:], in_=scr_i[:, 128:129])
            VF(v, v.tensor_copy(out=scr_f[:], in_=scr_i[:, 0:128]))
            v.nop().then_inc(s_cb, 1)
            VF(v, v.tensor_scalar(out=idn[:], in0=scr_f[:], scalar1=rowv[:],
                                  scalar2=None, op0=ALU.is_equal))
            v.tensor_scalar(out=ltri[:], in0=scr_f[:], scalar1=rowv[:],
                            scalar2=None, op0=ALU.is_gt)
            v.tensor_copy(out=idn_f[:], in_=idn[:])
            v.memset(ones_col[:], 1.0)
            v.memset(ones_rf[:], 1.0)
            v.memset(zeros16[:], 0.0)
            v.memset(eps_col[:], EPS).then_inc(s_cb, 1)

        @blk.tensor
        def _(t):
            t.wait_ge(s_cb, 2)
            t.wait_ge(s_cst, 16 * 4)
            t.transpose(out=pB0[:, 0:48], in_=btile[0:48, :],
                        identity=idn_f[0:48, 0:48]).then_inc(s_ct, 1)

        @blk.scalar
        def _(s):
            s.wait_ge(s_ct, 1)
            s.copy(out=bqc[:], in_=pB0[:, 0:16])
            s.copy(out=bkc[:], in_=pB0[:, 16:32])
            s.copy(out=bvc[:], in_=pB0[:, 32:48]).then_inc(s_pa, 1)

        @blk.sync
        def _(sync):
            sync.wait_ge(s_pa, 1)
            sync.dma_start(out=btile[0:64, :], in_=b1.rearrange("(c p) -> c p", p=128)).then_inc(s_cst, 16)

        @blk.tensor
        def _(t):
            t.wait_ge(s_cst, 16 * 5)
            t.transpose(out=pB0[:, 64:128], in_=btile[0:64, :],
                        identity=idn_f[0:64, 0:64]).then_inc(s_ct, 1)

        @blk.scalar
        def _(s):
            s.wait_ge(s_ct, 2)
            s.copy(out=b1c[:], in_=pB0[:, 64:128]).then_inc(s_pa, 1)

    def ph_p0(st):
        xt = [st.enter_context(nc.sbuf_tensor(f"xt{i}", [128, D], F32)) for i in range(2)]
        xnf = [st.enter_context(nc.sbuf_tensor(f"xnf{i}", [128, D], F32)) for i in range(2)]
        xnb = [st.enter_context(nc.sbuf_tensor(f"xnb{i}", [128, D], BF16)) for i in range(2)]
        junk = st.enter_context(nc.sbuf_tensor("junk", [128, D], BF16))
        wr_bc = st.enter_context(nc.sbuf_tensor("wr_bc", [128, D], F32))
        st24 = st.enter_context(nc.sbuf_tensor("st24", [128, 24], F32))

        @blk.sync
        def _(sync):
            sync.dma_start(out=wr_bc[:],
                           in_=wr.rearrange("a b -> (a b)").partition_broadcast(128)).then_inc(s_cst, 16)
            for t in range(32):
                if t >= 2:
                    sync.wait_ge(s_d1, t - 1)
                    sync.wait_ge(s_a2, t - 1)
                sync.dma_start(out=xt[t % 2][:],
                               in_=x[t * 128:(t + 1) * 128, :]).then_inc(s_x, 16)
                if t >= 2:
                    sync.wait_ge(s_d3, t - 1)
                    sync.dma_start(out=xnd[(t - 2) * 128:(t - 1) * 128, :],
                                   in_=xnb[t % 2][:]).then_inc(s_xnst, 16)
            for t in range(30, 32):
                sync.wait_ge(s_d3, t + 1)
                sync.dma_start(out=xnd[t * 128:(t + 1) * 128, :],
                               in_=xnb[t % 2][:]).then_inc(s_xnst, 16)

        @blk.vector
        def _(v):
            v.wait_ge(s_cst, 16 * 6)
            v.wait_ge(s_cb, 2)
            for t in range(32):
                v.wait_ge(s_x, 16 * (t + 1))
                for i in range(3):
                    v.bn_stats(out=st24[:, 6 * i:6 * i + 6],
                               in_=xt[t % 2][:, 512 * i:512 * (i + 1)])
                VF(v, v.bn_stats(out=st24[:, 18:24], in_=xt[t % 2][:, 1536:2048]))
                v.bn_aggr(out=mv[t % 2][:], in_=st24[:]).then_inc(s_d1, 1)
                v.wait_ge(s_a1, t + 1)
                VF(v, v.reciprocal(out=rstd_all[:, t:t + 1], in_=std_c[t % 2][:]))
                v.tensor_copy(out=mean_all[:, t:t + 1], in_=mv[t % 2][:, 0:1])
                v.scalar_tensor_tensor(out=nb_all[:, t:t + 1], in0=mv[t % 2][:, 0:1],
                                       scalar=-1.0, in1=rstd_all[:, t:t + 1],
                                       op0=ALU.mult, op1=ALU.mult).then_inc(s_d2, 1)
                v.wait_ge(s_a2, t + 1)
                if t >= 2:
                    v.wait_ge(s_xnst, 16 * (t - 1))
                v.scalar_tensor_tensor(out=junk[:], in0=xnf[t % 2][:], scalar=1.0,
                                       in1=wr_bc[:], op0=ALU.mult, op1=ALU.mult,
                                       accum_out=w_all[:, t:t + 1])
                v.tensor_copy(out=xnb[t % 2][:], in_=xnf[t % 2][:]).then_inc(s_d3, 1)
            v.wait_ge(s_d3, 32)
            v.tensor_scalar(out=w_all[:], in0=w_all[:], scalar1=br_col[:],
                            scalar2=None, op0=ALU.add).then_inc(s_d3, 1)

        @blk.scalar
        def _(s):
            for t in range(32):
                s.wait_ge(s_d1, t + 1)
                s.activation(out=std_c[t % 2][:], in_=mv[t % 2][:, 1:2],
                             func=AF.Sqrt, bias=eps_col[:]).then_inc(s_a1, 1)
                s.wait_ge(s_d2, t + 1)
                s.wait_ge(s_x, 16 * (t + 1))
                if t >= 2:
                    s.wait_ge(s_d3, t - 1)
                s.activation(out=xnf[t % 2][:], in_=xt[t % 2][:], func=AF.Identity,
                             bias=nb_all[:, t:t + 1],
                             scale=rstd_all[:, t:t + 1]).then_inc(s_a2, 1)


    def ph_p2(st, early):
        xnt = early["xnt"]
        kst = early["kst"]

        def tr_chunk(eng, n_src, buf):
            for c in range(16):
                eng.dma_start(out=xnt[buf][:, c, :],
                              in_=xnd[n_src * 512:(n_src + 1) * 512,
                                      c * 128:(c + 1) * 128],
                              transpose=True).then_inc(s_tr, 16)

        def q_store(sync, n):
            for m in range(16):
                idx = n * 16 + m
                sync.wait_ge(s_qe, idx + 1)
                sync.dma_start(out=qtd[m * 128:(m + 1) * 128, n * 512:(n + 1) * 512],
                               in_=kst[idx % 16][:]).then_inc(s_ks, 16)

        def kv_store(sync, i, n):
            for j in range(32):
                idx = 64 + i * 32 + j
                m = j if j < 16 else j - 16
                dst = ktd if j < 16 else vtd
                sync.wait_ge(s_qe, idx + 1)
                sync.dma_start(out=dst[m * 128:(m + 1) * 128, n * 512:(n + 1) * 512],
                               in_=kst[idx % 16][:]).then_inc(s_ks, 16)

        with contextlib.ExitStack() as st2:
            wqs = early["wqs"]

            @blk.gpsimd
            def _(g):
                g.dma_start(out=wqs[:],
                            in_=wq.rearrange("(c p) d -> p c d", p=128)).then_inc(s_wl, 16)
                g.iota(early["iota_i"][:], pattern=[[1, NSLOT]], base=0,
                       channel_multiplier=0).then_inc(s_gio, 1)

            @blk.sync
            def _(sync):
                sync.wait_ge(s_xnst, 16 * 32)
                for n in range(4):
                    if n >= 2:
                        sync.wait_ge(s_qm, 16 * (n - 1))
                    tr_chunk(sync, n, n % 2)
                    if n >= 1:
                        q_store(sync, n - 1)

            def q_evac(s, n):
                for m in range(16):
                    idx = n * 16 + m
                    s.wait_ge(s_qm, idx + 1)
                    if idx >= 16:
                        s.wait_ge(s_ks, 16 * (idx - 15))
                    s.activation(out=kst[idx % 16][:], in_=pA[idx % 3][:, 0:512],
                                 func=AF.Identity, bias=bqc[:, m:m + 1],
                                 scale=1.0).then_inc(s_qe, 1)

            @blk.scalar
            def _(s):
                for n in range(4):
                    q_evac(s, n)

            @blk.tensor
            def _(t):
                t.wait_ge(s_wl, 16)
                for n in range(4):
                    t.wait_ge(s_tr, 256 * (n + 1))
                    for m in range(16):
                        idx = n * 16 + m
                        pt = pA[idx % 3][:, 0:512]
                        if idx >= 3:
                            t.wait_ge(s_qe, idx - 2)
                        for c in range(16):
                            mm = t.matmul(out=pt, lhsT=wqs[:, c, m * 128:(m + 1) * 128],
                                          rhs=xnt[n % 2][:, c, :],
                                          start=(c == 0), stop=(c == 15))
                        mm.then_inc(s_qm, 1)

            @blk.sync
            def _(sync):
                q_store(sync, 3)

        KVORD = [2, 3, 0, 1, 4, 5, 6, 7]
        with contextlib.ExitStack() as st2:
            early["stq"].close()
            wks = st2.enter_context(nc.sbuf_tensor("wks", [128, 16, 2048], BF16))
            wvs = st2.enter_context(nc.sbuf_tensor("wvs", [128, 16, 2048], BF16))

            @blk.gpsimd
            def _(g):
                g.wait_ge(s_qm, 64)
                g.wait_ge(s_d3, 33)
                for wsl, dst in ((wk, wks), (wv, wvs)):
                    for hh in range(2):
                        g.dma_start(out=dst[:, :, hh * 1024:(hh + 1) * 1024],
                                    in_=wsl.rearrange("(c p) d -> p c d", p=128)
                                    [:, :, hh * 1024:(hh + 1) * 1024]).then_inc(s_wl, 16)

            @blk.sync
            def _(sync):
                for i, n in enumerate(KVORD):
                    if i >= 2:
                        sync.wait_ge(s_qm, 64 + 32 * (i - 1))
                        tr_chunk(sync, n, i % 2)
                    if i >= 1:
                        kv_store(sync, i - 1, KVORD[i - 1])

            cnt_ps = pB0[0:1, 0:1]
            cb_ps = pB0[0:128, 256:257]
            mbc_ps = pB0[0:128, 16:32]
            c16_ps = pB0[0:1, 300:316]

            def srch_act(s, k):
                if k == 4:
                    s.wait_ge(s_bp, 1)
                    SF(s, s.activation(out=cnt_sb[:], in_=cnt_ps, func=AF.Sqrt,
                                       scale=1.0))
                    SF(s, s.activation(out=cnt_sb[:], in_=cnt_sb[:], func=AF.Sqrt,
                                       scale=1.0))
                    s.activation(out=cnt_sb[:], in_=cnt_sb[:], func=AF.Sqrt,
                                 scale=1.0).then_inc(s_ba, 1)
                elif k >= 8 and k < 8 + 9 * NBIS and (k - 8) % 9 == 0:
                    r = (k - 8) // 9
                    s.wait_ge(s_bp, r + 3)
                    s.copy(out=early["mbs"][:], in_=mbc_ps).then_inc(s_ba, 1)

            def kv_evac(s, i):
                for j in range(32):
                    idx = 64 + i * 32 + j
                    m = j if j < 16 else j - 16
                    bias = bkc if j < 16 else bvc
                    s.wait_ge(s_qm, idx + 1)
                    s.wait_ge(s_ks, 16 * (idx - 15))
                    s.activation(out=kst[idx % 16][:], in_=pA[idx % 3][:, 0:512],
                                 func=AF.Identity, bias=bias[:, m:m + 1],
                                 scale=1.0).then_inc(s_qe, 1)
                    srch_act(s, i * 32 + j)

            @blk.scalar
            def _(s):
                for i in range(8):
                    kv_evac(s, i)

            def srch_pe(t, k):
                if k == 2:
                    t.wait_ge(s_bd, 1)
                    t.matmul(out=cnt_ps, lhsT=part_b[:], rhs=ones_col[:],
                             start=True, stop=True,
                             skip_group_check=True).then_inc(s_bp, 1)
                elif k >= 6 and k < 6 + 9 * NBIS and (k - 6) % 9 == 0:
                    r = (k - 6) // 9
                    t.wait_ge(s_bm, r + 1)
                    t.matmul(out=mbc_ps, lhsT=ones_rf[:], rhs=early["mrow"][:],
                             start=True, stop=True,
                             skip_group_check=True).then_inc(s_bp, 2 if r == 0 else 1)
                elif k >= 12 and k < 12 + 9 * NBIS and (k - 12) % 9 == 0:
                    r = (k - 12) // 9
                    t.wait_ge(s_bd, r + 2)
                    t.matmul(out=c16_ps, lhsT=ones_col[:], rhs=early["p16b"][:],
                             start=True, stop=True,
                             skip_group_check=True).then_inc(s_bc, 1)
                elif k == 9 * NBIS + 6:
                    t.wait_ge(s_bm, NBIS + 1)
                    t.matmul(out=cb_ps, lhsT=ones_rf[:], rhs=early["hir"][:],
                             start=True, stop=True,
                             skip_group_check=True).then_inc(s_bp, 1)
                elif k == 9 * NBIS + 18:
                    t.wait_ge(s_m1, 2)
                    t.matmul(out=cb_ps, lhsT=ltri[:], rhs=rtot_b[:], start=True,
                             stop=True, skip_group_check=True).then_inc(s_pm, 1)

            @blk.tensor
            def _(t):
                for i, n in enumerate(KVORD):
                    if i >= 2:
                        t.wait_ge(s_tr, 256 * (i + 3))
                    for j in range(32):
                        idx = 64 + i * 32 + j
                        m = j if j < 16 else j - 16
                        ws = wks if j < 16 else wvs
                        if i == 0:
                            t.wait_ge(s_wl, (32 if j < 16 else 64) +
                                      (0 if m < 8 else 16))
                        srch_pe(t, i * 32 + j)
                        pt = pA[idx % 3][:, 0:512]
                        if idx >= 3:
                            t.wait_ge(s_qe, idx - 2)
                        for c in range(16):
                            mm = t.matmul(out=pt, lhsT=ws[:, c, m * 128:(m + 1) * 128],
                                          rhs=xnt[i % 2][:, c, :],
                                          start=(c == 0), stop=(c == 15))
                        mm.then_inc(s_qm, 1)

            @blk.sync
            def _(sync):
                kv_store(sync, 7, KVORD[7])

    def ph_p2b(st, early):
        psb = st.enter_context(nc.sbuf_tensor("psb", [128, 16, NSLOT], BF16))
        iota384 = early["iota384"]
        iota1r = early["iota1r"]
        mrow = early["mrow"]
        brow = early["brow"]
        lor = early["lor"]
        hir = early["hir"]
        hrow = early["hrow"]
        t16a = early["t16a"]
        t16b = early["t16b"]
        s1r = early["s1r"]
        s2r = early["s2r"]
        q1r = early["q1r"]
        q2r = early["q2r"]
        mbs = early["mbs"]
        cscr = early["cscr"]
        p16 = early["p16"]
        p16b = early["p16b"]
        iota_i = early["iota_i"]
        psbf = st.enter_context(nc.sbuf_tensor("psbf", [128, 16, NSLOT], F32))
        xloc = st.enter_context(nc.sbuf_tensor("xloc", [128, 16, 2048], BF16))
        ptp = st.enter_context(nc.psum_tensor("ptp", [128, 128], BF16))

        cnt_ps = pB0[0:1, 0:1]
        cb_ps = pB0[0:128, 256:257]
        mbc_ps = pB0[0:128, 16:32]
        c16_ps = pB0[0:1, 300:316]

        @blk.vector
        def _(v):
            v.wait_ge(s_gio, 3)
            v.tensor_copy(out=iota384[:], in_=iota_i[:])
            VF(v, v.tensor_scalar(out=iota1r[:], in0=iota384[0:1, 0:16],
                                  scalar1=1.0, scalar2=None, op0=ALU.add))
            v.wait_ge(s_d3, 33)
            VF(v, v.tensor_tensor(out=cmp_t[:], in0=w_all[:], in1=w_all[:], op=ALU.mult))
            VF(v, v.tensor_tensor(out=cmp_t[:], in0=cmp_t[:], in1=cmp_t[:], op=ALU.mult))
            VF(v, v.tensor_tensor(out=cmp_t[:], in0=cmp_t[:], in1=cmp_t[:], op=ALU.mult))
            VF(v, v.reduce_sum(out=part_c[:], in_=cmp_t[:], axis=AX.X))
            v.tensor_copy(out=part_b[:], in_=part_c[:]).then_inc(s_bd, 1)


        @blk.vector
        def _(v):
            v.wait_ge(s_ba, 1)
            v.memset(brow[0:1, 0:1], 1.0)
            v.memset(brow[0:1, 17:18], 0.0)
            VF(v, v.tensor_copy(out=hir[:], in_=cnt_sb[:]))
            VF(v, v.tensor_scalar(out=lor[:], in0=hir[:], scalar1=-1.0,
                                  scalar2=None, op0=ALU.mult))
            VF(v, v.tensor_scalar(out=hrow[:], in0=hir[:], scalar1=2.0 / 17.0,
                                  scalar2=None, op0=ALU.mult))
            v.tensor_scalar(out=mrow[:], in0=iota1r[:], scalar1=hrow[:],
                            scalar2=lor[:], op0=ALU.mult,
                            op1=ALU.add).then_inc(s_bm, 1)

        for r in range(NBIS):
          with nc.named_scope(f"P2b_srch{r}"):
              @blk.vector
              def _(v, r=r):
                  v.wait_ge(s_ba, r + 2)
                  VF(v, v.tensor_tensor(
                      out=cscr[:],
                      in0=w_all[:].rearrange("p (a f) -> p a f", a=1).broadcast_to([128, 16, 32]),
                      in1=mbs[:].rearrange("p (f a) -> p f a", a=1).broadcast_to([128, 16, 32]),
                      op=ALU.is_lt))
                  VF(v, v.tensor_reduce(out=p16[:], in_=cscr[:], axis=AX.X, op=ALU.add))
                  v.tensor_copy(out=p16b[:], in_=p16[:]).then_inc(s_bd, 1)

              @blk.tensor
              def _(t, r=r):
                  t.wait_ge(s_bd, r + 2)
                  t.matmul(out=c16_ps, lhsT=ones_col[:], rhs=p16b[:],
                           start=True, stop=True,
                           skip_group_check=True).then_inc(s_bc, 1)

              @blk.vector
              def _(v, r=r):
                  v.wait_ge(s_bc, r + 1)
                  VF(v, v.tensor_single_scalar(out=brow[0:1, 1:17], in_=c16_ps,
                                               scalar=3584.5, op=ALU.is_lt))
                  v.tensor_tensor(out=t16a[:], in0=brow[0:1, 1:17],
                                  in1=brow[0:1, 2:18], op=ALU.subtract)
                  VF(v, v.tensor_tensor(out=t16b[:], in0=brow[0:1, 0:16],
                                        in1=brow[0:1, 1:17], op=ALU.subtract))
                  v.tensor_tensor(out=t16a[:], in0=t16a[:], in1=mrow[:], op=ALU.mult)
                  VF(v, v.tensor_tensor(out=t16b[:], in0=t16b[:], in1=mrow[:], op=ALU.mult))
                  v.tensor_reduce(out=s1r[:], in_=t16a[:], axis=AX.X, op=ALU.add)
                  VF(v, v.tensor_reduce(out=s2r[:], in_=t16b[:], axis=AX.X, op=ALU.add))
                  v.tensor_scalar(out=q1r[:], in0=brow[0:1, 1:2], scalar1=-1.0,
                                  scalar2=1.0, op0=ALU.mult, op1=ALU.add)
                  VF(v, v.tensor_tensor(out=q2r[:], in0=brow[0:1, 16:17],
                                        in1=hir[:], op=ALU.mult))
                  VF(v, v.tensor_tensor(out=q1r[:], in0=q1r[:], in1=lor[:], op=ALU.mult))
                  VF(v, v.tensor_tensor(out=lor[:], in0=s1r[:], in1=q1r[:], op=ALU.add))
                  VF(v, v.tensor_tensor(out=hir[:], in0=s2r[:], in1=q2r[:], op=ALU.add))
                  if r < NBIS - 1:
                      VF(v, v.tensor_tensor(out=hrow[:], in0=hir[:], in1=lor[:],
                                            op=ALU.subtract))
                      VF(v, v.tensor_scalar(out=hrow[:], in0=hrow[:], scalar1=1.0 / 17.0,
                                            scalar2=None, op0=ALU.mult))
                      v.tensor_scalar(out=mrow[:], in0=iota1r[:], scalar1=hrow[:],
                                      scalar2=lor[:], op0=ALU.mult,
                                      op1=ALU.add).then_inc(s_bm, 1)
                  else:
                      v.nop().then_inc(s_bm, 1)

        @blk.vector
        def _(v):
            v.wait_ge(s_bp, NBIS + 3)
            v.tensor_copy(out=hi_c[:], in_=cb_ps).then_inc(s_m1, 1)
        @blk.vector
        def _(v):
            v.wait_ge(s_m1, 1)
            VF(v, v.tensor_scalar(out=mask_t[:], in0=w_all[:, 0:16], scalar1=hi_c[:],
                                  scalar2=None, op0=ALU.is_gt))
            VF(v, v.tensor_scalar(out=notm_t[:], in0=mask_t[:], scalar1=-1.0, scalar2=1.0,
                                  op0=ALU.mult, op1=ALU.add))
            v.tensor_tensor(out=a1_t[:], in0=rstd_all[:, 0:16], in1=notm_t[:], op=ALU.mult)
            v.tensor_tensor(out=b1p_t[:], in0=nb_all[:, 0:16], in1=notm_t[:], op=ALU.mult)
            v.tensor_tensor(out=gw_t[:], in0=w_all[:, 0:16], in1=mask_t[:], op=ALU.mult)
            VF(v, v.tensor_tensor_scan(out=incl_t[:], data0=mask_t[:], data1=zeros16[:],
                                       initial=0.0, op0=ALU.add, op1=ALU.add))
            v.tensor_scalar(out=a1_t[:], in0=a1_t[:], scalar1=1.0, scalar2=None, op0=ALU.add)
            VF(v, v.tensor_tensor(out=rank_t[:], in0=incl_t[:], in1=mask_t[:], op=ALU.subtract))
            VF(v, v.reduce_sum(out=part_c[:], in_=mask_t[:], axis=AX.X))
            v.tensor_copy(out=rtot_b[:], in_=part_c[:]).then_inc(s_m1, 1)

        @blk.vector
        def _(v):
            v.wait_ge(s_pm, 1)
            v.tensor_copy(out=offs_c[:], in_=cb_ps).then_inc(s_m1, 1)
            v.wait_ge(s_m1, 3)
            v.tensor_scalar(out=rank_t[:], in0=rank_t[:], scalar1=offs_c[:],
                            scalar2=None, op0=ALU.add).then_inc(s_m1, 1)
            v.wait_ge(s_m1, 4)
            v.wait_ge(s_ks, 16 * 320)
            for t_ in range(15):
                v.tensor_scalar(out=psbf[:, t_, :], in0=iota384[:],
                                scalar1=rank_t[:, t_:t_ + 1], scalar2=None,
                                op0=ALU.is_equal)
            VF(v, v.tensor_scalar(out=psbf[:, 15, :], in0=iota384[:],
                                  scalar1=rank_t[:, 15:16], scalar2=None,
                                  op0=ALU.is_equal))
            for t_ in range(16):
                inst = v.tensor_scalar(out=psbf[:, t_, :], in0=psbf[:, t_, :],
                                       scalar1=mask_t[:, t_:t_ + 1], scalar2=None,
                                       op0=ALU.mult)
            VF(v, inst)
            for t_ in range(16):
                inst = v.tensor_copy(out=psb[:, t_, :], in_=psbf[:, t_, :])
            inst.then_inc(s_m1, 1)

        @blk.tensor
        def _(t):
            t.wait_ge(s_m1, 5)
            for t_ in range(16):
                for r in range(3):
                    ix = t_ * 3 + r
                    if ix >= 1:
                        t.wait_ge(s_pa, 2 + ix)
                    t.transpose(out=ptp[:], in_=psb[:, t_, r * 128:(r + 1) * 128],
                                identity=idn[:]).then_inc(s_pm, 1)

        @blk.scalar
        def _(s):
            for t_ in range(16):
                for r in range(3):
                    ix = t_ * 3 + r
                    s.wait_ge(s_pm, 2 + ix)
                    s.copy(out=late['pt_sb'][:, r, t_, :], in_=ptp[:]).then_inc(s_pa, 1)

        @blk.sync
        def _(sync):
            sync.wait_ge(s_ks, 16 * 320)
            for t_ in range(8):
                sync.dma_start(out=xloc[:, t_, :],
                               in_=xnd[t_ * 128:(t_ + 1) * 128, :]).then_inc(s_gl, 16)

        @blk.scalar
        def _(s):
            s.wait_ge(s_ks, 16 * 320)
            for t_ in range(8, 16):
                s.dma_start(out=xloc[:, t_, :],
                            in_=xnd[t_ * 128:(t_ + 1) * 128, :]).then_inc(s_gl, 16)

        @blk.tensor
        def _(t):
            t.wait_ge(s_gl, 256)
            for c in range(16):
                gb = pA[c % 2][:, 0:NSLOT]
                if c >= 2:
                    t.wait_ge(s_pa, 49 + c)
                for t_ in range(16):
                    mm = t.matmul(out=gb, lhsT=xloc[:, t_, c * 128:(c + 1) * 128],
                                  rhs=psb[:, t_, :], start=(t_ == 0), stop=(t_ == 15))
                mm.then_inc(s_pm, 1)
            for r in range(3):
                if r >= 1:
                    t.wait_ge(s_m1, 5 + r)
                gb = pA[2][:, 0:1]
                for t_ in range(16):
                    mm = t.matmul(out=gb, lhsT=psbf[:, t_, r * 128:(r + 1) * 128],
                                  rhs=gw_t[:, t_:t_ + 1], start=(t_ == 0), stop=(t_ == 15))
                mm.then_inc(s_pm, 1)

        @blk.scalar
        def _(s):
            for c in range(16):
                s.wait_ge(s_pm, 50 + c)
                s.copy(out=late['xgt'][:, c, :], in_=pA[c % 2][:, 0:NSLOT]).then_inc(s_pa, 1)

        @blk.vector
        def _(v):
            for r in range(3):
                v.wait_ge(s_pm, 66 + r)
                v.tensor_copy(out=late['gwcol'][:, r:r + 1], in_=pA[2][:, 0:1]).then_inc(s_m1, 1)

    def ph_p4(st):
        ht = st.enter_context(nc.sbuf_tensor("ht", [128, 64, NSLOT], BF16))
        b2_bc = st.enter_context(nc.sbuf_tensor("b2_bc", [128, D], F32))
        u_f = [st.enter_context(nc.sbuf_tensor(f"u{i}", [128, NSLOT], F32)) for i in range(2)]
        g2 = [st.enter_context(nc.sbuf_tensor(f"g2_{i}", [128, NSLOT], F32)) for i in range(2)]
        g3 = [st.enter_context(nc.sbuf_tensor(f"g3_{i}", [128, NSLOT], F32)) for i in range(2)]
        tn = [st.enter_context(nc.sbuf_tensor(f"tn{i}", [128, NSLOT], F32)) for i in range(2)]

        with contextlib.ExitStack() as st2:
            w1s = [st2.enter_context(nc.sbuf_tensor(f"w1s{i}", [128, 16, 512], BF16)) for i in range(2)]

            @blk.gpsimd
            def _(g):
                g.wait_ge(s_pm, 68)
                g.dma_start(out=b2_bc[:], in_=b2.partition_broadcast(128)).then_inc(s_bb, 16)
                for j in range(16):
                    if j >= 2:
                        g.wait_ge(s_f1, 4 * (j - 1))
                    g.dma_start(out=w1s[j % 2][:],
                                in_=w1[:, j * 512:(j + 1) * 512]
                                .rearrange("(c p) d -> p c d", p=128)).then_inc(s_wl, 16)

            @blk.tensor
            def _(t):
                t.wait_ge(s_pa, 66)
                for j in range(16):
                    t.wait_ge(s_wl, 80 + 16 * (j + 1))
                    for m2 in range(4):
                        ix = j * 4 + m2
                        ub = pA[ix % 3][:, 0:NSLOT]
                        if ix >= 3:
                            t.wait_ge(s_f2, 2 * (ix - 3) + 1)
                        for c in range(16):
                            mm = t.matmul(out=ub, lhsT=w1s[j % 2][:, c, m2 * 128:(m2 + 1) * 128],
                                          rhs=late['xgt'][:, c, :], start=(c == 0), stop=(c == 15))
                        mm.then_inc(s_f1, 1)

            @blk.scalar
            def _(s):
                for ix in range(64):
                    s.wait_ge(s_f1, ix + 1)
                    if ix >= 2:
                        s.wait_ge(s_f3, 2 * ix - 2)
                    s.activation(out=u_f[ix % 2][:], in_=pA[ix % 3][:, 0:NSLOT],
                                 func=AF.Identity, bias=b1c[:, ix:ix + 1],
                                 scale=1.0).then_inc(s_f2, 1)
                    s.wait_ge(s_f3, 2 * ix + 1)
                    s.activation(out=tn[ix % 2][:], in_=tn[ix % 2][:], func=AF.Tanh,
                                 scale=0.7978845608028654).then_inc(s_f2, 1)

            @blk.vector
            def _(v):
                for ix in range(64):
                    v.wait_ge(s_f2, 2 * ix + 1)
                    u = u_f[ix % 2]
                    VF(v, v.tensor_tensor(out=g2[ix % 2][:], in0=u[:], in1=u[:], op=ALU.mult))
                    VF(v, v.tensor_tensor(out=g3[ix % 2][:], in0=g2[ix % 2][:], in1=u[:], op=ALU.mult))
                    v.scalar_tensor_tensor(out=tn[ix % 2][:], in0=g3[ix % 2][:],
                                           scalar=0.044715, in1=u[:],
                                           op0=ALU.mult, op1=ALU.add).then_inc(s_f3, 1)
                    v.wait_ge(s_f2, 2 * ix + 2)
                    VF(v, v.tensor_scalar(out=g2[ix % 2][:], in0=tn[ix % 2][:], scalar1=0.5,
                                          scalar2=0.5, op0=ALU.mult, op1=ALU.add))
                    v.tensor_tensor(out=ht[:, ix, :], in0=u[:], in1=g2[ix % 2][:],
                                    op=ALU.mult).then_inc(s_f3, 1)

        with contextlib.ExitStack() as st2:
            w2s = [st2.enter_context(nc.sbuf_tensor(f"w2s{i}", [128, 64, 256], BF16)) for i in range(2)]

            @blk.gpsimd
            def _(g):
                g.wait_ge(s_f1, 64)
                for j in range(8):
                    if j >= 2:
                        g.wait_ge(s_f4, 3 * (j - 1))
                    g.dma_start(out=w2s[j % 2][:],
                                in_=w2[:, j * 256:(j + 1) * 256]
                                .rearrange("(c p) d -> p c d", p=128)).then_inc(s_wl, 16)

            @blk.tensor
            def _(t):
                t.wait_ge(s_f3, 128)
                for j in range(8):
                    t.wait_ge(s_wl, 336 + 16 * (j + 1))
                    for r in range(3):
                        ix = j * 3 + r
                        vb = pA[ix % 3][:, 512:768]
                        if ix >= 3:
                            t.wait_ge(s_f5, ix - 2)
                        for c in range(64):
                            mm = t.matmul(out=vb, lhsT=ht[:, c, r * 128:(r + 1) * 128],
                                          rhs=w2s[j % 2][:, c, :], start=(c == 0),
                                          stop=(c == 63))
                        mm.then_inc(s_f4, 1)

            @blk.vector
            def _(v):
                v.wait_ge(s_m1, 8)
                v.wait_ge(s_bb, 16)
                for j in range(8):
                    for r in range(3):
                        ix = j * 3 + r
                        sl2 = slice(j * 256, (j + 1) * 256)
                        v.wait_ge(s_f4, ix + 1)
                        VF(v, v.tensor_tensor(out=tn[0][:, 0:256], in0=pA[ix % 3][:, 512:768],
                                              in1=b2_bc[:, sl2], op=ALU.add))
                        VF(v, v.tensor_scalar(out=tn[1][:, 0:256], in0=tn[0][:, 0:256],
                                              scalar1=late['gwcol'][:, r:r + 1],
                                              scalar2=None, op0=ALU.mult))
                        VF(v, v.tensor_copy(out=late['valsg'][:, r, sl2], in_=tn[1][:, 0:256]))
                        VF(v, v.tensor_copy(out=g2[0][:, 0:256], in_=late['valsg'][:, r, sl2]))
                        VF(v, v.tensor_tensor(out=g3[0][:, 0:256], in0=tn[1][:, 0:256],
                                              in1=g2[0][:, 0:256], op=ALU.subtract))
                        v.tensor_copy(out=late['valsl'][:, r, sl2],
                                      in_=g3[0][:, 0:256]).then_inc(s_f5, 1)

    def ph_p3(st):
        esb = st.enter_context(nc.sbuf_tensor("esb", [128, 16, 1024], BF16))
        qt = [st.enter_context(nc.sbuf_tensor(f"qt{i}", [128, SL], BF16)) for i in range(2)]
        kt = [st.enter_context(nc.sbuf_tensor(f"kt{i}", [128, S], BF16)) for i in range(2)]
        vt = [st.enter_context(nc.sbuf_tensor(f"vt{i}", [128, 32, 128], BF16)) for i in range(2)]
        rd_sb = st.enter_context(nc.sbuf_tensor("rd_sb", [1, 1024], F32))
        ftmp = st.enter_context(nc.sbuf_tensor("ftmp", [128, 512], BF16))
        den = st.enter_context(nc.psum_tensor("den", [128, 512], F32))

        @blk.sync
        def _(sync):
            sync.wait_ge(s_ks, 16 * 320)
            sync.wait_ge(s_f4, 24)
            for p in range(NPAIR):
                if p >= 2:
                    sync.wait_ge(s_av, (p - 1) * 16)
                sync.dma_start(out=qt[p % 2][:],
                               in_=qtd[p * 128:(p + 1) * 128, :]).then_inc(s_pl, 16)
                sync.dma_start(out=kt[p % 2][:],
                               in_=ktd[p * 128:(p + 1) * 128, :]).then_inc(s_pl, 16)
                for c in range(32):
                    sync.dma_start(out=vt[p % 2][:, c, :],
                                   in_=vtd[p * 128:(p + 1) * 128, c * 128:(c + 1) * 128],
                                   transpose=True).then_inc(s_pl, 16)

        def emit_ftmp(s, hb):
            s.wait_ge(s_av, (hb + 1) * 4)
            if hb > 0:
                s.wait_ge(s_fn, hb)
            s.copy(out=ftmp[:], in_=pB0[:]).then_inc(s_fe, 1)

        def emit_exps(s, p):
            for b in range(4):
                hb = p * 4 + b
                for q in range(4):
                    eg = hb * 4 + q
                    for j in range(8):
                        nsc = eg * 8 + j + 1
                        s.wait_ge(s_sc, nsc)
                        if j == 0 and eg > 1:
                            s.wait_ge(s_av, eg - 1)
                        s.activation(out=esb[:, (eg % 2) * 8 + j, :],
                                     in_=pA[(nsc - 1) % 2][:],
                                     func=AF.Exp, scale=SCALE).then_inc(s_ex, 1)
                        if q == 0 and j == 6 and hb > 0:
                            emit_ftmp(s, hb - 1)

        @blk.scalar
        def _(s):
            for p in range(NPAIR):
                emit_exps(s, p)
            emit_ftmp(s, 63)

        def ep_bcast(t, hb):
            t.wait_ge(s_rc, hb + 1)
            if hb >= 1:
                t.wait_ge(s_fn, hb)
            t.matmul(out=pA[2][0:64, 512:1024], lhsT=ones_rf[0:1, 0:64],
                     rhs=rd_sb[0:1, 0:512], start=True, stop=True,
                     tile_position=(0, 0), skip_group_check=True)
            t.matmul(out=pA[2][64:128, 512:1024], lhsT=ones_rf[0:1, 0:64],
                     rhs=rd_sb[0:1, 512:1024], start=True, stop=True,
                     tile_position=(0, 64),
                     skip_group_check=True).then_inc(s_bcm, 1)

        def phase2_chunk(t, gm, j):
            p, q = gm // 16, gm % 4
            dn = den if (gm // 4) % 2 == 0 else pA[2]
            c = q * 8 + j
            es = (gm % 2) * 8 + j
            first = (q == 0 and j == 0)
            last = (q == 3 and j == 7)
            t.matmul(out=pB0[0:64, :], lhsT=vt[p % 2][:, c, 0:64],
                     rhs=esb[:, es, 0:512], start=first, stop=last,
                     tile_position=(0, 0), skip_group_check=True)
            t.matmul(out=pB0[64:128, :], lhsT=vt[p % 2][:, c, 64:128],
                     rhs=esb[:, es, 512:1024], start=first, stop=last,
                     tile_position=(0, 64), skip_group_check=True)
            t.matmul(out=dn[0:1, 0:512], lhsT=ones_col[:],
                     rhs=esb[:, es, 0:512], start=first, stop=last,
                     tile_position=(0, 0), skip_group_check=True)
            mm = t.matmul(out=dn[64:65, 0:512], lhsT=ones_col[:],
                          rhs=esb[:, es, 512:1024], start=first, stop=last,
                          tile_position=(0, 64), skip_group_check=True)
            if j == 7:
                mm.then_inc(s_av, 1)

        @blk.tensor
        def _(t):
            for g in range(64 * 4):
                p, b, q = g // 16, (g // 4) % 4, g % 4
                hb = g // 4
                if g % 16 == 0:
                    t.wait_ge(s_pl, 34 * 16 * (p + 1))
                for j in range(8):
                    c = q * 8 + j
                    nsc = g * 8 + j + 1
                    if q == 2 and j == 4 and hb > 0:
                        ep_bcast(t, hb - 1)
                    if nsc > 2:
                        t.wait_ge(s_ex, nsc - 2)
                    t.matmul(out=pA[(nsc - 1) % 2][:, 0:512],
                             lhsT=kt[p % 2][0:64, c * 128:(c + 1) * 128],
                             rhs=qt[p % 2][0:64, b * 512:(b + 1) * 512],
                             start=True, stop=True, tile_position=(0, 0))
                    t.matmul(out=pA[(nsc - 1) % 2][:, 512:1024],
                             lhsT=kt[p % 2][64:128, c * 128:(c + 1) * 128],
                             rhs=qt[p % 2][64:128, b * 512:(b + 1) * 512],
                             start=True, stop=True,
                             tile_position=(64, 0)).then_inc(s_sc, 1)
                    if g >= 1:
                        if (g - 1) % 4 == 0 and j == 0:
                            hbm = (g - 1) // 4
                            t.wait_ge(s_fe, hbm)
                            if hbm >= 2:
                                t.wait_ge(s_rc, hbm - 1)
                        phase2_chunk(t, g - 1, j)
            t.wait_ge(s_ex, 64 * 4 * 8)
            for j in range(8):
                phase2_chunk(t, 64 * 4 - 1, j)
            ep_bcast(t, 63)

        @blk.vector
        def _(v):
            for p in range(NPAIR):
                for b in range(4):
                    hb = p * 4 + b
                    v.wait_ge(s_av, (hb + 1) * 4)
                    if hb >= 1:
                        v.wait_ge(s_bcm, hb)
                    dn = den if hb % 2 == 0 else pA[2]
                    v.reciprocal(out=rd_sb[0:1, 0:512], in_=dn[0:1, 0:512])
                    v.reciprocal(out=rd_sb[0:1, 512:1024],
                                 in_=dn[64:65, 0:512]).then_inc(s_rc, 1)
                    v.wait_ge(s_bcm, hb + 1)
                    v.wait_ge(s_fe, hb + 1)
                    v.tensor_tensor(out=late['featsT'][:, p, b * 512:(b + 1) * 512],
                                    in0=ftmp[:], in1=pA[2][:, 512:1024],
                                    op=ALU.mult).then_inc(s_fn, 1)

    def ph_p5(st):
        wos = late["wos"]
        pt5 = st.enter_context(nc.sbuf_tensor("pt5", [128, 3, 16, 128], BF16))
        bo_bc = st.enter_context(nc.sbuf_tensor("bo_bc", [128, D], F32))
        xt2 = [st.enter_context(nc.sbuf_tensor(f"xt2_{i}", [128, 1024], F32)) for i in range(2)]
        atmp = st.enter_context(nc.sbuf_tensor("atmp", [128, 512], F32))
        stg = [st.enter_context(nc.sbuf_tensor(f"stg{i}", [128, 1024], F32)) for i in range(2)]

        for half in range(2):
            @blk.gpsimd
            def _(g, half=half):
                if half == 1:
                    g.wait_ge(s_om, 32)
                if half == 0:
                    g.wait_ge(s_f4, 24)
                g.dma_start(out=wos[:],
                            in_=wo[:, half * 1024:(half + 1) * 1024]
                            .rearrange("(c p) d -> p c d", p=128)).then_inc(s_wl, 16)
                if half == 0:
                    g.wait_ge(s_bcm, 64)
                    g.dma_start(out=bo_bc[:], in_=bo.partition_broadcast(128)).then_inc(s_bb, 16)

            @blk.sync
            def _(sync, half=half):
                if half == 0:
                    sync.wait_ge(s_bcm, 64)
                    sync.wait_ge(s_fn, 64)
                    sync.wait_ge(s_ptd, 16)
                    sync.dma_start(out=pt5[:],
                                   in_=ptd.rearrange("p (a b c) -> p a b c",
                                                     a=3, b=16)).then_inc(s_ptd, 16)
                for m in range(16):
                    ix = half * 16 + m
                    if ix >= 2:
                        sync.wait_ge(s_oa, 2 * (ix - 1))
                    sync.dma_start(out=xt2[m % 2][:],
                                   in_=x[m * 128:(m + 1) * 128,
                                         half * 1024:(half + 1) * 1024]).then_inc(s_x2, 16)
                    if m >= 2:
                        sync.wait_ge(s_oa, 2 * (ix - 1))
                        sync.dma_start(out=out[(m - 2) * 128:(m - 1) * 128,
                                               half * 1024:(half + 1) * 1024],
                                       in_=stg[m % 2][:]).then_inc(s_os, 16)
                for m in range(14, 16):
                    ix = half * 16 + m
                    sync.wait_ge(s_oa, 2 * (ix + 1))
                    sync.dma_start(out=out[m * 128:(m + 1) * 128,
                                           half * 1024:(half + 1) * 1024],
                                   in_=stg[m % 2][:]).then_inc(s_os, 16)

            @blk.tensor
            def _(t, half=half):
                t.wait_ge(s_wl, 464 + 16 * (half + 1))
                if half == 0:
                    t.wait_ge(s_fn, 64)
                    t.wait_ge(s_f5, 24)
                    t.wait_ge(s_ptd, 32)
                for m in range(16):
                    for n2 in range(2):
                        ix = (half * 16 + m) * 2 + n2
                        ob = pA[ix % 3][:, 0:512]
                        if ix >= 3:
                            t.wait_ge(s_oa, ix - 2)
                        for pr in range(16):
                            t.matmul(out=ob, lhsT=late['featsT'][:, pr, m * 128:(m + 1) * 128],
                                     rhs=wos[:, pr, n2 * 512:(n2 + 1) * 512],
                                     start=(pr == 0), stop=False,
                                     skip_group_check=True)
                        for r in range(3):
                            t.matmul(out=ob, lhsT=pt5[:, r, m, :],
                                     rhs=late['valsg'][:, r, half * 1024 + n2 * 512:
                                               half * 1024 + (n2 + 1) * 512],
                                     start=False, stop=False,
                                     skip_group_check=True)
                        for r in range(3):
                            mm = t.matmul(out=ob, lhsT=pt5[:, r, m, :],
                                          rhs=late['valsl'][:, r, half * 1024 + n2 * 512:
                                                    half * 1024 + (n2 + 1) * 512],
                                          start=False, stop=(r == 2),
                                          skip_group_check=True)
                        mm.then_inc(s_om, 1)

            @blk.vector
            def _(v, half=half):
                if half == 0:
                    v.wait_ge(s_bb, 32)
                for m in range(16):
                    for n2 in range(2):
                        ix = (half * 16 + m) * 2 + n2
                        v.wait_ge(s_om, ix + 1)
                        v.wait_ge(s_x2, 16 * (half * 16 + m + 1))
                        if half * 16 + m >= 2:
                            v.wait_ge(s_os, 16 * (half * 16 + m - 1))
                        sl = slice(n2 * 512, (n2 + 1) * 512)
                        fsl = slice(half * 1024 + n2 * 512, half * 1024 + (n2 + 1) * 512)
                        v.tensor_scalar(out=atmp[:], in0=xt2[m % 2][:, sl],
                                        scalar1=a1_t[:, m:m + 1],
                                        scalar2=b1p_t[:, m:m + 1],
                                        op0=ALU.mult, op1=ALU.add)
                        VF(v, v.tensor_tensor(out=stg[m % 2][:, sl],
                                              in0=pA[ix % 3][:, 0:512],
                                              in1=bo_bc[:, fsl], op=ALU.add))
                        v.tensor_tensor(out=stg[m % 2][:, sl], in0=stg[m % 2][:, sl],
                                        in1=atmp[:], op=ALU.add).then_inc(s_oa, 1)


        @blk.sync
        def _(sync):
            sync.wait_ge(s_os, 16 * 32)

    with contextlib.ExitStack() as st, nc.named_scope("Pc"):
        ph_consts(st)
    with contextlib.ExitStack() as stw:
        stq = contextlib.ExitStack()
        early = dict(
            xnt=[stw.enter_context(nc.sbuf_tensor(f"xnt{i}", [128, 16, 512], BF16)) for i in range(2)],
            kst=[stw.enter_context(nc.sbuf_tensor(f"kst{i}", [128, 512], BF16)) for i in range(16)],
        )
        for nm, shp, dt in [("iota_i", [128, NSLOT], I32), ("iota384", [128, NSLOT], F32),
                            ("iota1r", [1, 16], F32), ("mrow", [1, 16], F32),
                            ("brow", [1, 18], F32), ("lor", [1, 1], F32),
                            ("hir", [1, 1], F32), ("hrow", [1, 1], F32),
                            ("t16a", [1, 16], F32), ("t16b", [1, 16], F32),
                            ("s1r", [1, 1], F32), ("s2r", [1, 1], F32),
                            ("q1r", [1, 1], F32), ("q2r", [1, 1], F32),
                            ("mbs", [128, 16], F32), ("cscr", [128, 16, 32], F32),
                            ("p16", [128, 16], F32), ("p16b", [128, 16], BF16)]:
            early[nm] = stw.enter_context(nc.sbuf_tensor(nm, shp, dt))
        early["wqs"] = stq.enter_context(nc.sbuf_tensor("wqs", [128, 16, 2048], BF16))
        early["stq"] = stq
        if upto >= 0:
            with contextlib.ExitStack() as st, nc.named_scope("P0"):
                ph_p0(st)
        if upto >= 1:
            with contextlib.ExitStack() as st, nc.named_scope("P2"):
                ph_p2(st, early)
    late["gwcol"] = A("gwcol", [128, 3], F32)
    late["valsg"] = A("valsg", [128, 3, D], BF16)
    late["valsl"] = A("valsl", [128, 3, D], BF16)
    stx = contextlib.ExitStack()
    late["xgt"] = stx.enter_context(nc.sbuf_tensor("xgt", [128, 16, NSLOT], BF16))
    stp = contextlib.ExitStack()
    late["pt_sb"] = stp.enter_context(nc.sbuf_tensor("pt_sb", [128, 3, 16, 128], BF16))
    if upto >= 2:
        with contextlib.ExitStack() as st, nc.named_scope("P2b"):
            ph_p2b(st, early)

        @blk.sync
        def _(sync):
            sync.wait_ge(s_pa, 50)
            sync.dma_start(out=ptd,
                           in_=late["pt_sb"].ap().rearrange("p a b c -> p (a b c)")
                           ).then_inc(s_ptd, 16)
    stp.close()
    if upto >= 3:
        with contextlib.ExitStack() as st, nc.named_scope("P4"):
            ph_p4(st)
        if dbg:
            xgtd = nc.dram_tensor("xgtd", [128, 16, NSLOT], BF16, kind="Internal").ap()

            @blk.sync
            def _(sync):
                sync.wait_ge(s_pa, 66)
                sync.dma_start(out=xgtd, in_=late["xgt"][:])
    stx.close()
    late["featsT"] = A("featsT", [128, 16, SL], BF16)
    late["wos"] = A("wos", [128, 16, 1024], BF16)
    if upto >= 4:
        with contextlib.ExitStack() as st, nc.named_scope("P3"):
            ph_p3(st)
    if upto >= 5:
        with contextlib.ExitStack() as st, nc.named_scope("P5"):
            ph_p5(st)

    if dbg:
        dоко = None
        dbg_done = SEM("s_dbg")
        dbgs = []

        def dump_dram(name, src_ap, shape, dt):
            o = nc.dram_tensor(name, shape, dt, kind="ExternalOutput").ap()
            dbgs.append((o, src_ap))

        @blk.sync
        def _(sync):
            if upto >= 0:
                sync.wait_ge(s_xnst, 16 * 32)
            if upto >= 1:
                sync.wait_ge(s_ks, 16 * 320)
            if upto >= 2:
                sync.wait_ge(s_m1, 8)
                sync.wait_ge(s_pa, 66)
            if upto >= 3:
                sync.wait_ge(s_f5, 24)
            if upto >= 4:
                sync.wait_ge(s_fn, 64)
            n = 0
            def dump(name, src_ap, shape, dt):
                nonlocal n
                o = nc.dram_tensor(name, shape, dt, kind="ExternalOutput").ap()
                sync.dma_start(out=o, in_=src_ap).then_inc(dbg_done, 16)
                n += 1
            if upto >= 0:
                dump("d_xn", xnd, [S, D], BF16)
                dump("d_w", w_all[:], [128, 32], F32)
                dump("d_mean", mean_all[:], [128, 32], F32)
                dump("d_rstd", rstd_all[:], [128, 32], F32)
            if upto >= 1:
                dump("d_qt", qtd, [D, SL], BF16)
                dump("d_kt", ktd, [D, S], BF16)
                dump("d_vt", vtd, [D, S], BF16)
            if upto >= 2:
                dump("d_hi", hi_c[:], [128, 1], F32)
                dump("d_mask", mask_t[:], [128, 16], F32)
                dump("d_rank", rank_t[:], [128, 16], F32)
                dump("d_xgt", xgtd, [128, 16, NSLOT], BF16)
                dump("d_gwc", late["gwcol"][:], [128, 3], F32)
                dump("d_pt", ptd, [128, 3, 16, 128], BF16)
            if upto >= 3:
                dump("d_vals", late["valsg"][:], [128, 3, D], BF16)
            if upto >= 4:
                dump("d_ft", late["featsT"][:], [128, 16, SL], BF16)
            sync.wait_ge(dbg_done, 16 * n)

    es.close()
    return nc


_NC_CACHE = {}


def _get_nc():
    if "nc" not in _NC_CACHE:
        _NC_CACHE["nc"] = build_nc()
    return _NC_CACHE["nc"]


def make_in_maps(L0_attn_x, L0_attn_Wq, L0_attn_bq, L0_attn_Wk, L0_attn_bk,
                 L0_attn_Wv, L0_attn_bv, L0_attn_Wo, L0_attn_bo,
                 L0_ffn_W_r, L0_ffn_b_r, L0_ffn_W1, L0_ffn_b1, L0_ffn_W2,
                 L0_ffn_b2, L0_ffn_position_ids=None, L0_ffn_cache_position=None,
                 **_unused):
    x = np.ascontiguousarray(np.asarray(L0_attn_x, dtype=np.float32))
    consts = dict(
        wq=np.ascontiguousarray(np.asarray(L0_attn_Wq, np.float32)),
        wk=np.ascontiguousarray(np.asarray(L0_attn_Wk, np.float32)),
        wv=np.ascontiguousarray(np.asarray(L0_attn_Wv, np.float32)),
        wo=np.ascontiguousarray(np.asarray(L0_attn_Wo, np.float32)),
        bq=np.ascontiguousarray(np.asarray(L0_attn_bq, np.float32).reshape(D)),
        bk=np.ascontiguousarray(np.asarray(L0_attn_bk, np.float32).reshape(D)),
        bv=np.ascontiguousarray(np.asarray(L0_attn_bv, np.float32).reshape(D)),
        bo=np.ascontiguousarray(np.asarray(L0_attn_bo, np.float32).reshape(D)),
        wr=np.ascontiguousarray(np.asarray(L0_ffn_W_r, np.float32).reshape(D, 1)),
        br=np.ascontiguousarray(np.asarray(L0_ffn_b_r, np.float32).reshape(1)),
        w1=np.ascontiguousarray(np.asarray(L0_ffn_W1, np.float32)),
        b1=np.ascontiguousarray(np.asarray(L0_ffn_b1, np.float32).reshape(DFF)),
        w2=np.ascontiguousarray(np.asarray(L0_ffn_W2, np.float32)),
        b2=np.ascontiguousarray(np.asarray(L0_ffn_b2, np.float32).reshape(D)),
    )
    in_maps = []
    for core in range(8):
        b, h = core // 2, core % 2
        m = dict(consts)
        m["x"] = np.ascontiguousarray(np.roll(x[b], -h * SL, axis=0))
        in_maps.append(m)
    return in_maps


def kernel(**inputs):
    in_maps = make_in_maps(**inputs)
    nc = _get_nc()
    res = run_bass_kernel_spmd(nc, in_maps, core_ids=list(range(8)))
    outp = np.empty((4, 4096, 2048), dtype=np.float32)
    for core in range(8):
        b, h = core // 2, core % 2
        outp[b, h * SL:(h + 1) * SL] = res.results[core]["out"]
    return outp


# version 21
# speedup vs baseline: 1.0571x; 1.0136x over previous
import math
import contextlib
import numpy as np
import ml_dtypes
import concourse.bass as bass
import concourse.mybir as mybir
from concourse.bass_utils import run_bass_kernel_spmd

F32 = mybir.dt.float32
BF16 = mybir.dt.bfloat16
I32 = mybir.dt.int32
AF = mybir.ActivationFunctionType
ALU = mybir.AluOpType
AX = mybir.AxisListType

S = 4096
SL = 2048
D = 2048
DFF = 8192
NPAIR = 16
NSLOT = 384
SCALE = 1.0 / math.sqrt(32.0)
NBIS = 10
EPS = 1e-5


def build_nc(upto=99, dbg=False):
    nc = bass.Bass("TRN2", target_bir_lowering=False,
                   detect_race_conditions=False)

    x = nc.dram_tensor("x", [S, D], F32, kind="ExternalInput").ap()
    wq = nc.dram_tensor("wq", [D, D], F32, kind="ExternalInput").ap()
    wk = nc.dram_tensor("wk", [D, D], F32, kind="ExternalInput").ap()
    wv = nc.dram_tensor("wv", [D, D], F32, kind="ExternalInput").ap()
    wo = nc.dram_tensor("wo", [D, D], F32, kind="ExternalInput").ap()
    bq = nc.dram_tensor("bq", [D], F32, kind="ExternalInput").ap()
    bk = nc.dram_tensor("bk", [D], F32, kind="ExternalInput").ap()
    bv = nc.dram_tensor("bv", [D], F32, kind="ExternalInput").ap()
    bo = nc.dram_tensor("bo", [D], F32, kind="ExternalInput").ap()
    wr = nc.dram_tensor("wr", [D, 1], F32, kind="ExternalInput").ap()
    br = nc.dram_tensor("br", [1], F32, kind="ExternalInput").ap()
    w1 = nc.dram_tensor("w1", [D, DFF], F32, kind="ExternalInput").ap()
    b1 = nc.dram_tensor("b1", [DFF], F32, kind="ExternalInput").ap()
    w2 = nc.dram_tensor("w2", [DFF, D], F32, kind="ExternalInput").ap()
    b2 = nc.dram_tensor("b2", [D], F32, kind="ExternalInput").ap()
    out = nc.dram_tensor("out", [SL, D], F32, kind="ExternalOutput").ap()

    xnd = nc.dram_tensor("xnd", [S, D], BF16, kind="Internal").ap()
    qtd = nc.dram_tensor("qtd", [D, SL], BF16, kind="Internal").ap()
    ktd = nc.dram_tensor("ktd", [D, S], BF16, kind="Internal").ap()
    vtd = nc.dram_tensor("vtd", [D, S], BF16, kind="Internal").ap()
    ptd = nc.dram_tensor("ptd", [128, 3 * 16 * 128], BF16, kind="Internal").ap()

    A = nc.alloc_sbuf_tensor
    SEM = nc.alloc_semaphore

    idn = A("idn", [128, 128], BF16)
    idn_f = A("idn_f", [128, 128], F32)
    ltri = A("ltri", [128, 128], BF16)
    ones_col = A("ones_col", [128, 1], BF16)
    ones_rf = A("ones_rf", [1, 128], F32)

    mean_all = A("mean_all", [128, 32], F32)
    rstd_all = A("rstd_all", [128, 32], F32)
    nb_all = A("nb_all", [128, 32], F32)
    w_all = A("w_all", [128, 32], F32)
    mv = [A(f"mv{i}", [128, 2], F32) for i in range(2)]
    std_c = [A(f"std{i}", [128, 1], F32) for i in range(2)]

    br_col = A("br_col", [128, 1], F32)
    bqc = A("bqc", [128, 16], F32)
    bkc = A("bkc", [128, 16], F32)
    bvc = A("bvc", [128, 16], F32)
    b1c = A("b1c", [128, 64], F32)

    lo_c = A("lo_c", [128, 1], F32)
    hi_c = A("hi_c", [128, 1], F32)
    mid_c = A("mid_c", [128, 1], F32)
    cnd_c = A("cnd_c", [128, 1], F32)
    d1_c = A("d1_c", [128, 1], F32)
    d2_c = A("d2_c", [128, 1], F32)
    cmp_t = A("cmp_t", [128, 32], F32)
    part_c = A("part_c", [128, 1], F32)
    part_b = A("part_b", [128, 1], BF16)
    cnt_sb = A("cnt_sb", [1, 1], F32)

    mask_t = A("mask_t", [128, 16], F32)
    notm_t = A("notm_t", [128, 16], F32)
    a1_t = A("a1_t", [128, 16], F32)
    b1p_t = A("b1p_t", [128, 16], F32)
    gw_t = A("gw_t", [128, 16], F32)
    incl_t = A("incl_t", [128, 16], F32)
    rank_t = A("rank_t", [128, 16], F32)
    rtot_b = A("rtot_b", [128, 1], BF16)
    offs_c = A("offs_c", [128, 1], F32)
    zeros16 = A("zeros16", [128, 16], F32)
    eps_col = A("eps_col", [128, 1], F32)

    late = {}

    pA = [nc.alloc_psum_tensor(f"pA{i}", [128, 1024], F32) for i in range(3)]
    pB0 = nc.alloc_psum_tensor("pB0", [128, 512], F32)

    s_gio = SEM("s_gio"); s_cst = SEM("s_cst"); s_cb = SEM("s_cb")
    s_ct = SEM("s_ct"); s_x = SEM("s_x"); s_d1 = SEM("s_d1")
    s_a1 = SEM("s_a1"); s_d2 = SEM("s_d2"); s_a2 = SEM("s_a2")
    s_d3 = SEM("s_d3"); s_xnst = SEM("s_xnst"); s_wl = SEM("s_wl")
    s_tr = SEM("s_tr"); s_qm = SEM("s_qm"); s_qe = SEM("s_qe")
    s_ks = SEM("s_ks"); s_xf = SEM("s_xf")
    s_bm = SEM("s_bm"); s_bd = SEM("s_bd"); s_bp = SEM("s_bp")
    s_ba = SEM("s_ba"); s_bc = SEM("s_bc")
    s_m1 = SEM("s_m1"); s_pm = SEM("s_pm"); s_pa = SEM("s_pa"); s_ptd = SEM("s_ptd")
    s_gl = SEM("s_gl")
    s_f1 = SEM("s_f1"); s_f2 = SEM("s_f2"); s_f3 = SEM("s_f3")
    s_f4 = SEM("s_f4"); s_f5 = SEM("s_f5")
    s_pl = SEM("s_pl"); s_pl2 = SEM("s_pl2"); s_sc = SEM("s_sc")
    s_ex = SEM("s_ex"); s_av = SEM("s_av"); s_rc = SEM("s_rc")
    s_bcm = SEM("s_bcm"); s_fe = SEM("s_fe"); s_fn = SEM("s_fn")
    s_x2 = SEM("s_x2"); s_om = SEM("s_om"); s_oa = SEM("s_oa")
    s_bb = SEM("s_bb"); s_fl = SEM("s_fl"); s_fla = SEM("s_fla")
    flc = [0]
    flca = [0]

    def VF(v, inst):
        inst.then_inc(s_fl, 1)
        flc[0] += 1
        v.wait_ge(s_fl, flc[0])

    def SF(s, inst):
        inst.then_inc(s_fla, 1)
        flca[0] += 1
        s.wait_ge(s_fla, flca[0])
    s_os = SEM("s_os")

    es = contextlib.ExitStack()
    blk = es.enter_context(nc.Block())

    def ph_consts(st):
        scr_i = st.enter_context(nc.sbuf_tensor("scr_i", [128, NSLOT], I32))
        scr_f = st.enter_context(nc.sbuf_tensor("scr_f", [128, 128], F32))
        rowv = st.enter_context(nc.sbuf_tensor("rowv", [128, 1], F32))
        btile = st.enter_context(nc.sbuf_tensor("btile", [128, 128], F32))
        @blk.gpsimd
        def _(g):
            g.iota(scr_i[:, 0:128], pattern=[[1, 128]], base=0,
                   channel_multiplier=0).then_inc(s_gio, 1)
            g.iota(scr_i[:, 128:129], pattern=[[0, 1]], base=0,
                   channel_multiplier=1).then_inc(s_gio, 1)

        @blk.sync
        def _(sync):
            sync.dma_start(out=br_col[:], in_=br.partition_broadcast(128)).then_inc(s_cst, 16)
            sync.dma_start(out=btile[0:16, :], in_=bq.rearrange("(c p) -> c p", p=128)).then_inc(s_cst, 16)
            sync.dma_start(out=btile[16:32, :], in_=bk.rearrange("(c p) -> c p", p=128)).then_inc(s_cst, 16)
            sync.dma_start(out=btile[32:48, :], in_=bv.rearrange("(c p) -> c p", p=128)).then_inc(s_cst, 16)

        @blk.vector
        def _(v):
            v.wait_ge(s_gio, 2)
            v.tensor_copy(out=rowv[:], in_=scr_i[:, 128:129])
            VF(v, v.tensor_copy(out=scr_f[:], in_=scr_i[:, 0:128]))
            v.nop().then_inc(s_cb, 1)
            VF(v, v.tensor_scalar(out=idn[:], in0=scr_f[:], scalar1=rowv[:],
                                  scalar2=None, op0=ALU.is_equal))
            v.tensor_scalar(out=ltri[:], in0=scr_f[:], scalar1=rowv[:],
                            scalar2=None, op0=ALU.is_gt)
            v.tensor_copy(out=idn_f[:], in_=idn[:])
            v.memset(ones_col[:], 1.0)
            v.memset(ones_rf[:], 1.0)
            v.memset(zeros16[:], 0.0)
            v.memset(eps_col[:], EPS).then_inc(s_cb, 1)

        @blk.tensor
        def _(t):
            t.wait_ge(s_cb, 2)
            t.wait_ge(s_cst, 16 * 4)
            t.transpose(out=pB0[:, 0:48], in_=btile[0:48, :],
                        identity=idn_f[0:48, 0:48]).then_inc(s_ct, 1)

        @blk.scalar
        def _(s):
            s.wait_ge(s_ct, 1)
            s.copy(out=bqc[:], in_=pB0[:, 0:16])
            s.copy(out=bkc[:], in_=pB0[:, 16:32])
            s.copy(out=bvc[:], in_=pB0[:, 32:48]).then_inc(s_pa, 1)

        @blk.sync
        def _(sync):
            sync.wait_ge(s_pa, 1)
            sync.dma_start(out=btile[0:64, :], in_=b1.rearrange("(c p) -> c p", p=128)).then_inc(s_cst, 16)

        @blk.tensor
        def _(t):
            t.wait_ge(s_cst, 16 * 5)
            t.transpose(out=pB0[:, 64:128], in_=btile[0:64, :],
                        identity=idn_f[0:64, 0:64]).then_inc(s_ct, 1)

        @blk.scalar
        def _(s):
            s.wait_ge(s_ct, 2)
            s.copy(out=b1c[:], in_=pB0[:, 64:128]).then_inc(s_pa, 1)

    def ph_p0(st):
        xt = [st.enter_context(nc.sbuf_tensor(f"xt{i}", [128, D], F32)) for i in range(2)]
        xnf = [st.enter_context(nc.sbuf_tensor(f"xnf{i}", [128, D], F32)) for i in range(2)]
        xnb = [st.enter_context(nc.sbuf_tensor(f"xnb{i}", [128, D], BF16)) for i in range(2)]
        junk = st.enter_context(nc.sbuf_tensor("junk", [128, D], BF16))
        wr_bc = st.enter_context(nc.sbuf_tensor("wr_bc", [128, D], F32))
        st24 = st.enter_context(nc.sbuf_tensor("st24", [128, 24], F32))

        @blk.sync
        def _(sync):
            sync.dma_start(out=wr_bc[:],
                           in_=wr.rearrange("a b -> (a b)").partition_broadcast(128)).then_inc(s_cst, 16)
            for t in range(32):
                if t >= 2:
                    sync.wait_ge(s_d1, t - 1)
                    sync.wait_ge(s_a2, t - 1)
                sync.dma_start(out=xt[t % 2][:],
                               in_=x[t * 128:(t + 1) * 128, :]).then_inc(s_x, 16)
                if t >= 2:
                    sync.wait_ge(s_d3, t - 1)
                    sync.dma_start(out=xnd[(t - 2) * 128:(t - 1) * 128, :],
                                   in_=xnb[t % 2][:]).then_inc(s_xnst, 16)
            for t in range(30, 32):
                sync.wait_ge(s_d3, t + 1)
                sync.dma_start(out=xnd[t * 128:(t + 1) * 128, :],
                               in_=xnb[t % 2][:]).then_inc(s_xnst, 16)

        @blk.vector
        def _(v):
            v.wait_ge(s_cst, 16 * 6)
            v.wait_ge(s_cb, 2)
            for t in range(32):
                v.wait_ge(s_x, 16 * (t + 1))
                for i in range(3):
                    v.bn_stats(out=st24[:, 6 * i:6 * i + 6],
                               in_=xt[t % 2][:, 512 * i:512 * (i + 1)])
                VF(v, v.bn_stats(out=st24[:, 18:24], in_=xt[t % 2][:, 1536:2048]))
                v.bn_aggr(out=mv[t % 2][:], in_=st24[:]).then_inc(s_d1, 1)
                v.wait_ge(s_a1, t + 1)
                VF(v, v.reciprocal(out=rstd_all[:, t:t + 1], in_=std_c[t % 2][:]))
                v.tensor_copy(out=mean_all[:, t:t + 1], in_=mv[t % 2][:, 0:1])
                v.scalar_tensor_tensor(out=nb_all[:, t:t + 1], in0=mv[t % 2][:, 0:1],
                                       scalar=-1.0, in1=rstd_all[:, t:t + 1],
                                       op0=ALU.mult, op1=ALU.mult).then_inc(s_d2, 1)
                v.wait_ge(s_a2, t + 1)
                if t >= 2:
                    v.wait_ge(s_xnst, 16 * (t - 1))
                v.scalar_tensor_tensor(out=junk[:], in0=xnf[t % 2][:], scalar=1.0,
                                       in1=wr_bc[:], op0=ALU.mult, op1=ALU.mult,
                                       accum_out=w_all[:, t:t + 1])
                v.tensor_copy(out=xnb[t % 2][:], in_=xnf[t % 2][:]).then_inc(s_d3, 1)
            v.wait_ge(s_d3, 32)
            v.tensor_scalar(out=w_all[:], in0=w_all[:], scalar1=br_col[:],
                            scalar2=None, op0=ALU.add).then_inc(s_d3, 1)

        @blk.scalar
        def _(s):
            for t in range(32):
                s.wait_ge(s_d1, t + 1)
                s.activation(out=std_c[t % 2][:], in_=mv[t % 2][:, 1:2],
                             func=AF.Sqrt, bias=eps_col[:]).then_inc(s_a1, 1)
                s.wait_ge(s_d2, t + 1)
                s.wait_ge(s_x, 16 * (t + 1))
                if t >= 2:
                    s.wait_ge(s_d3, t - 1)
                s.activation(out=xnf[t % 2][:], in_=xt[t % 2][:], func=AF.Identity,
                             bias=nb_all[:, t:t + 1],
                             scale=rstd_all[:, t:t + 1]).then_inc(s_a2, 1)


    def ph_p2(st, early):
        xnt = early["xnt"]
        kst = early["kst"]

        def tr_chunk(eng, n_src, buf):
            for c in range(16):
                eng.dma_start(out=xnt[buf][:, c, :],
                              in_=xnd[n_src * 512:(n_src + 1) * 512,
                                      c * 128:(c + 1) * 128],
                              transpose=True).then_inc(s_tr, 16)

        def q_store(sync, n):
            for m in range(16):
                idx = n * 16 + m
                sync.wait_ge(s_qe, idx + 1)
                sync.dma_start(out=qtd[m * 128:(m + 1) * 128, n * 512:(n + 1) * 512],
                               in_=kst[idx % 16][:]).then_inc(s_ks, 16)

        def kv_store(sync, i, n):
            for j in range(32):
                idx = 64 + i * 32 + j
                m = j if j < 16 else j - 16
                dst = ktd if j < 16 else vtd
                sync.wait_ge(s_qe, idx + 1)
                sync.dma_start(out=dst[m * 128:(m + 1) * 128, n * 512:(n + 1) * 512],
                               in_=kst[idx % 16][:]).then_inc(s_ks, 16)

        with contextlib.ExitStack() as st2:
            wqs = early["wqs"]

            @blk.gpsimd
            def _(g):
                g.dma_start(out=wqs[:],
                            in_=wq.rearrange("(c p) d -> p c d", p=128)).then_inc(s_wl, 16)
                g.iota(early["iota_i"][:], pattern=[[1, NSLOT]], base=0,
                       channel_multiplier=0).then_inc(s_gio, 1)

            @blk.sync
            def _(sync):
                sync.wait_ge(s_xnst, 16 * 32)
                for n in range(4):
                    if n >= 2:
                        sync.wait_ge(s_qm, 16 * (n - 1))
                    tr_chunk(sync, n, n % 2)
                    if n >= 1:
                        q_store(sync, n - 1)

            def q_evac(s, n):
                for m in range(16):
                    idx = n * 16 + m
                    s.wait_ge(s_qm, idx + 1)
                    if idx >= 16:
                        s.wait_ge(s_ks, 16 * (idx - 15))
                    s.activation(out=kst[idx % 16][:], in_=pA[idx % 3][:, 0:512],
                                 func=AF.Identity, bias=bqc[:, m:m + 1],
                                 scale=1.0).then_inc(s_qe, 1)

            @blk.scalar
            def _(s):
                for n in range(4):
                    q_evac(s, n)

            @blk.tensor
            def _(t):
                t.wait_ge(s_wl, 16)
                for n in range(4):
                    t.wait_ge(s_tr, 256 * (n + 1))
                    for m in range(16):
                        idx = n * 16 + m
                        pt = pA[idx % 3][:, 0:512]
                        if idx >= 3:
                            t.wait_ge(s_qe, idx - 2)
                        for c in range(16):
                            mm = t.matmul(out=pt, lhsT=wqs[:, c, m * 128:(m + 1) * 128],
                                          rhs=xnt[n % 2][:, c, :],
                                          start=(c == 0), stop=(c == 15))
                        mm.then_inc(s_qm, 1)

            @blk.sync
            def _(sync):
                q_store(sync, 3)

        KVORD = [2, 3, 0, 1, 4, 5, 6, 7]
        with contextlib.ExitStack() as st2:
            early["stq"].close()
            wks = st2.enter_context(nc.sbuf_tensor("wks", [128, 16, 2048], BF16))
            wvs = st2.enter_context(nc.sbuf_tensor("wvs", [128, 16, 2048], BF16))

            @blk.gpsimd
            def _(g):
                g.wait_ge(s_qm, 64)
                g.wait_ge(s_d3, 33)
                for wsl, dst in ((wk, wks), (wv, wvs)):
                    for hh in range(2):
                        g.dma_start(out=dst[:, :, hh * 1024:(hh + 1) * 1024],
                                    in_=wsl.rearrange("(c p) d -> p c d", p=128)
                                    [:, :, hh * 1024:(hh + 1) * 1024]).then_inc(s_wl, 16)

            @blk.sync
            def _(sync):
                for i, n in enumerate(KVORD):
                    if i >= 2:
                        sync.wait_ge(s_qm, 64 + 32 * (i - 1))
                        tr_chunk(sync, n, i % 2)
                    if i >= 1:
                        kv_store(sync, i - 1, KVORD[i - 1])

            cnt_ps = pB0[0:1, 0:1]
            cb_ps = pB0[0:128, 256:257]
            mbc_ps = pB0[0:128, 16:32]
            c16_ps = pB0[0:1, 300:316]

            def srch_act(s, k):
                if k == 4:
                    s.wait_ge(s_bp, 1)
                    SF(s, s.activation(out=cnt_sb[:], in_=cnt_ps, func=AF.Sqrt,
                                       scale=1.0))
                    SF(s, s.activation(out=cnt_sb[:], in_=cnt_sb[:], func=AF.Sqrt,
                                       scale=1.0))
                    s.activation(out=cnt_sb[:], in_=cnt_sb[:], func=AF.Sqrt,
                                 scale=1.0).then_inc(s_ba, 1)
                elif k >= 8 and k < 8 + 9 * NBIS and (k - 8) % 9 == 0:
                    r = (k - 8) // 9
                    s.wait_ge(s_bp, r + 3)
                    s.copy(out=early["mbs"][:], in_=mbc_ps).then_inc(s_ba, 1)

            def kv_evac(s, i):
                for j in range(32):
                    idx = 64 + i * 32 + j
                    m = j if j < 16 else j - 16
                    bias = bkc if j < 16 else bvc
                    s.wait_ge(s_qm, idx + 1)
                    s.wait_ge(s_ks, 16 * (idx - 15))
                    s.activation(out=kst[idx % 16][:], in_=pA[idx % 3][:, 0:512],
                                 func=AF.Identity, bias=bias[:, m:m + 1],
                                 scale=1.0).then_inc(s_qe, 1)
                    srch_act(s, i * 32 + j)

            @blk.scalar
            def _(s):
                for i in range(8):
                    kv_evac(s, i)

            def srch_pe(t, k):
                if k == 2:
                    t.wait_ge(s_bd, 1)
                    t.matmul(out=cnt_ps, lhsT=part_b[:], rhs=ones_col[:],
                             start=True, stop=True,
                             skip_group_check=True).then_inc(s_bp, 1)
                elif k >= 6 and k < 6 + 9 * NBIS and (k - 6) % 9 == 0:
                    r = (k - 6) // 9
                    t.wait_ge(s_bm, r + 1)
                    t.matmul(out=mbc_ps, lhsT=ones_rf[:], rhs=early["mrow"][:],
                             start=True, stop=True,
                             skip_group_check=True).then_inc(s_bp, 2 if r == 0 else 1)
                elif k >= 12 and k < 12 + 9 * NBIS and (k - 12) % 9 == 0:
                    r = (k - 12) // 9
                    t.wait_ge(s_bd, r + 2)
                    t.matmul(out=c16_ps, lhsT=ones_col[:], rhs=early["p16b"][:],
                             start=True, stop=True,
                             skip_group_check=True).then_inc(s_bc, 1)
                elif k == 9 * NBIS + 6:
                    t.wait_ge(s_bm, NBIS + 1)
                    t.matmul(out=cb_ps, lhsT=ones_rf[:], rhs=early["hir"][:],
                             start=True, stop=True,
                             skip_group_check=True).then_inc(s_bp, 1)
                elif k == 9 * NBIS + 18:
                    t.wait_ge(s_m1, 2)
                    t.matmul(out=cb_ps, lhsT=ltri[:], rhs=rtot_b[:], start=True,
                             stop=True, skip_group_check=True).then_inc(s_pm, 1)

            @blk.tensor
            def _(t):
                for i, n in enumerate(KVORD):
                    if i >= 2:
                        t.wait_ge(s_tr, 256 * (i + 3))
                    for j in range(32):
                        idx = 64 + i * 32 + j
                        m = j if j < 16 else j - 16
                        ws = wks if j < 16 else wvs
                        if i == 0:
                            t.wait_ge(s_wl, (32 if j < 16 else 64) +
                                      (0 if m < 8 else 16))
                        srch_pe(t, i * 32 + j)
                        pt = pA[idx % 3][:, 0:512]
                        if idx >= 3:
                            t.wait_ge(s_qe, idx - 2)
                        for c in range(16):
                            mm = t.matmul(out=pt, lhsT=ws[:, c, m * 128:(m + 1) * 128],
                                          rhs=xnt[i % 2][:, c, :],
                                          start=(c == 0), stop=(c == 15))
                        mm.then_inc(s_qm, 1)

            @blk.sync
            def _(sync):
                kv_store(sync, 7, KVORD[7])

    def ph_p2b(st, early):
        psb = st.enter_context(nc.sbuf_tensor("psb", [128, 16, NSLOT], BF16))
        iota384 = early["iota384"]
        iota1r = early["iota1r"]
        mrow = early["mrow"]
        brow = early["brow"]
        lor = early["lor"]
        hir = early["hir"]
        hrow = early["hrow"]
        t16a = early["t16a"]
        t16b = early["t16b"]
        s1r = early["s1r"]
        s2r = early["s2r"]
        q1r = early["q1r"]
        q2r = early["q2r"]
        mbs = early["mbs"]
        cscr = early["cscr"]
        p16 = early["p16"]
        p16b = early["p16b"]
        iota_i = early["iota_i"]
        psbf = st.enter_context(nc.sbuf_tensor("psbf", [128, 16, NSLOT], F32))
        xloc = st.enter_context(nc.sbuf_tensor("xloc", [128, 16, 2048], BF16))
        ptp = st.enter_context(nc.psum_tensor("ptp", [128, 128], BF16))

        cnt_ps = pB0[0:1, 0:1]
        cb_ps = pB0[0:128, 256:257]
        mbc_ps = pB0[0:128, 16:32]
        c16_ps = pB0[0:1, 300:316]

        @blk.vector
        def _(v):
            v.wait_ge(s_gio, 3)
            v.tensor_copy(out=iota384[:], in_=iota_i[:])
            VF(v, v.tensor_scalar(out=iota1r[:], in0=iota384[0:1, 0:16],
                                  scalar1=1.0, scalar2=None, op0=ALU.add))
            v.wait_ge(s_d3, 33)
            VF(v, v.tensor_tensor(out=cmp_t[:], in0=w_all[:], in1=w_all[:], op=ALU.mult))
            VF(v, v.tensor_tensor(out=cmp_t[:], in0=cmp_t[:], in1=cmp_t[:], op=ALU.mult))
            VF(v, v.tensor_tensor(out=cmp_t[:], in0=cmp_t[:], in1=cmp_t[:], op=ALU.mult))
            VF(v, v.reduce_sum(out=part_c[:], in_=cmp_t[:], axis=AX.X))
            v.tensor_copy(out=part_b[:], in_=part_c[:]).then_inc(s_bd, 1)


        @blk.vector
        def _(v):
            v.wait_ge(s_ba, 1)
            v.memset(brow[0:1, 0:1], 1.0)
            v.memset(brow[0:1, 17:18], 0.0)
            VF(v, v.tensor_copy(out=hir[:], in_=cnt_sb[:]))
            VF(v, v.tensor_scalar(out=lor[:], in0=hir[:], scalar1=-1.0,
                                  scalar2=None, op0=ALU.mult))
            VF(v, v.tensor_scalar(out=hrow[:], in0=hir[:], scalar1=2.0 / 17.0,
                                  scalar2=None, op0=ALU.mult))
            v.tensor_scalar(out=mrow[:], in0=iota1r[:], scalar1=hrow[:],
                            scalar2=lor[:], op0=ALU.mult,
                            op1=ALU.add).then_inc(s_bm, 1)

        for r in range(NBIS):
          with nc.named_scope(f"P2b_srch{r}"):
              @blk.vector
              def _(v, r=r):
                  v.wait_ge(s_ba, r + 2)
                  VF(v, v.tensor_tensor(
                      out=cscr[:],
                      in0=w_all[:].rearrange("p (a f) -> p a f", a=1).broadcast_to([128, 16, 32]),
                      in1=mbs[:].rearrange("p (f a) -> p f a", a=1).broadcast_to([128, 16, 32]),
                      op=ALU.is_lt))
                  VF(v, v.tensor_reduce(out=p16[:], in_=cscr[:], axis=AX.X, op=ALU.add))
                  v.tensor_copy(out=p16b[:], in_=p16[:]).then_inc(s_bd, 1)

              @blk.tensor
              def _(t, r=r):
                  t.wait_ge(s_bd, r + 2)
                  t.matmul(out=c16_ps, lhsT=ones_col[:], rhs=p16b[:],
                           start=True, stop=True,
                           skip_group_check=True).then_inc(s_bc, 1)

              @blk.vector
              def _(v, r=r):
                  v.wait_ge(s_bc, r + 1)
                  VF(v, v.tensor_single_scalar(out=brow[0:1, 1:17], in_=c16_ps,
                                               scalar=3584.5, op=ALU.is_lt))
                  v.tensor_tensor(out=t16a[:], in0=brow[0:1, 1:17],
                                  in1=brow[0:1, 2:18], op=ALU.subtract)
                  VF(v, v.tensor_tensor(out=t16b[:], in0=brow[0:1, 0:16],
                                        in1=brow[0:1, 1:17], op=ALU.subtract))
                  v.tensor_tensor(out=t16a[:], in0=t16a[:], in1=mrow[:], op=ALU.mult)
                  VF(v, v.tensor_tensor(out=t16b[:], in0=t16b[:], in1=mrow[:], op=ALU.mult))
                  v.tensor_reduce(out=s1r[:], in_=t16a[:], axis=AX.X, op=ALU.add)
                  VF(v, v.tensor_reduce(out=s2r[:], in_=t16b[:], axis=AX.X, op=ALU.add))
                  v.tensor_scalar(out=q1r[:], in0=brow[0:1, 1:2], scalar1=-1.0,
                                  scalar2=1.0, op0=ALU.mult, op1=ALU.add)
                  VF(v, v.tensor_tensor(out=q2r[:], in0=brow[0:1, 16:17],
                                        in1=hir[:], op=ALU.mult))
                  VF(v, v.tensor_tensor(out=q1r[:], in0=q1r[:], in1=lor[:], op=ALU.mult))
                  VF(v, v.tensor_tensor(out=lor[:], in0=s1r[:], in1=q1r[:], op=ALU.add))
                  VF(v, v.tensor_tensor(out=hir[:], in0=s2r[:], in1=q2r[:], op=ALU.add))
                  if r < NBIS - 1:
                      VF(v, v.tensor_tensor(out=hrow[:], in0=hir[:], in1=lor[:],
                                            op=ALU.subtract))
                      VF(v, v.tensor_scalar(out=hrow[:], in0=hrow[:], scalar1=1.0 / 17.0,
                                            scalar2=None, op0=ALU.mult))
                      v.tensor_scalar(out=mrow[:], in0=iota1r[:], scalar1=hrow[:],
                                      scalar2=lor[:], op0=ALU.mult,
                                      op1=ALU.add).then_inc(s_bm, 1)
                  else:
                      v.nop().then_inc(s_bm, 1)

        @blk.vector
        def _(v):
            v.wait_ge(s_bp, NBIS + 3)
            v.tensor_copy(out=hi_c[:], in_=cb_ps).then_inc(s_m1, 1)
        @blk.vector
        def _(v):
            v.wait_ge(s_m1, 1)
            VF(v, v.tensor_scalar(out=mask_t[:], in0=w_all[:, 0:16], scalar1=hi_c[:],
                                  scalar2=None, op0=ALU.is_gt))
            VF(v, v.tensor_scalar(out=notm_t[:], in0=mask_t[:], scalar1=-1.0, scalar2=1.0,
                                  op0=ALU.mult, op1=ALU.add))
            v.tensor_tensor(out=a1_t[:], in0=rstd_all[:, 0:16], in1=notm_t[:], op=ALU.mult)
            v.tensor_tensor(out=b1p_t[:], in0=nb_all[:, 0:16], in1=notm_t[:], op=ALU.mult)
            v.tensor_tensor(out=gw_t[:], in0=w_all[:, 0:16], in1=mask_t[:], op=ALU.mult)
            VF(v, v.tensor_tensor_scan(out=incl_t[:], data0=mask_t[:], data1=zeros16[:],
                                       initial=0.0, op0=ALU.add, op1=ALU.add))
            v.tensor_scalar(out=a1_t[:], in0=a1_t[:], scalar1=1.0, scalar2=None, op0=ALU.add)
            VF(v, v.tensor_tensor(out=rank_t[:], in0=incl_t[:], in1=mask_t[:], op=ALU.subtract))
            VF(v, v.reduce_sum(out=part_c[:], in_=mask_t[:], axis=AX.X))
            v.tensor_copy(out=rtot_b[:], in_=part_c[:]).then_inc(s_m1, 1)

        @blk.vector
        def _(v):
            v.wait_ge(s_pm, 1)
            v.tensor_copy(out=offs_c[:], in_=cb_ps).then_inc(s_m1, 1)
            v.wait_ge(s_m1, 3)
            v.tensor_scalar(out=rank_t[:], in0=rank_t[:], scalar1=offs_c[:],
                            scalar2=None, op0=ALU.add).then_inc(s_m1, 1)
            v.wait_ge(s_m1, 4)
            v.wait_ge(s_ks, 16 * 320)
            for t_ in range(15):
                v.tensor_scalar(out=psbf[:, t_, :], in0=iota384[:],
                                scalar1=rank_t[:, t_:t_ + 1], scalar2=None,
                                op0=ALU.is_equal)
            VF(v, v.tensor_scalar(out=psbf[:, 15, :], in0=iota384[:],
                                  scalar1=rank_t[:, 15:16], scalar2=None,
                                  op0=ALU.is_equal))
            for t_ in range(16):
                inst = v.tensor_scalar(out=psbf[:, t_, :], in0=psbf[:, t_, :],
                                       scalar1=mask_t[:, t_:t_ + 1], scalar2=None,
                                       op0=ALU.mult)
            VF(v, inst)
            for t_ in range(16):
                inst = v.tensor_copy(out=psb[:, t_, :], in_=psbf[:, t_, :])
            inst.then_inc(s_m1, 1)

        @blk.tensor
        def _(t):
            t.wait_ge(s_m1, 5)
            for t_ in range(16):
                for r in range(3):
                    ix = t_ * 3 + r
                    if ix >= 1:
                        t.wait_ge(s_pa, 2 + ix)
                    t.transpose(out=ptp[:], in_=psb[:, t_, r * 128:(r + 1) * 128],
                                identity=idn[:]).then_inc(s_pm, 1)

        @blk.scalar
        def _(s):
            for t_ in range(16):
                for r in range(3):
                    ix = t_ * 3 + r
                    s.wait_ge(s_pm, 2 + ix)
                    s.copy(out=late['pt_sb'][:, r, t_, :], in_=ptp[:]).then_inc(s_pa, 1)

        @blk.sync
        def _(sync):
            sync.wait_ge(s_ks, 16 * 320)
            for t_ in range(8):
                sync.dma_start(out=xloc[:, t_, :],
                               in_=xnd[t_ * 128:(t_ + 1) * 128, :]).then_inc(s_gl, 16)

        @blk.scalar
        def _(s):
            s.wait_ge(s_ks, 16 * 320)
            for t_ in range(8, 16):
                s.dma_start(out=xloc[:, t_, :],
                            in_=xnd[t_ * 128:(t_ + 1) * 128, :]).then_inc(s_gl, 16)

        @blk.tensor
        def _(t):
            t.wait_ge(s_gl, 256)
            for c in range(16):
                gb = pA[c % 2][:, 0:NSLOT]
                if c >= 2:
                    t.wait_ge(s_pa, 49 + c)
                for t_ in range(16):
                    mm = t.matmul(out=gb, lhsT=xloc[:, t_, c * 128:(c + 1) * 128],
                                  rhs=psb[:, t_, :], start=(t_ == 0), stop=(t_ == 15))
                mm.then_inc(s_pm, 1)
            for r in range(3):
                if r >= 1:
                    t.wait_ge(s_m1, 5 + r)
                gb = pA[2][:, 0:1]
                for t_ in range(16):
                    mm = t.matmul(out=gb, lhsT=psbf[:, t_, r * 128:(r + 1) * 128],
                                  rhs=gw_t[:, t_:t_ + 1], start=(t_ == 0), stop=(t_ == 15))
                mm.then_inc(s_pm, 1)

        @blk.scalar
        def _(s):
            for c in range(16):
                s.wait_ge(s_pm, 50 + c)
                s.copy(out=late['xgt'][:, c, :], in_=pA[c % 2][:, 0:NSLOT]).then_inc(s_pa, 1)

        @blk.vector
        def _(v):
            for r in range(3):
                v.wait_ge(s_pm, 66 + r)
                v.tensor_copy(out=late['gwcol'][:, r:r + 1], in_=pA[2][:, 0:1]).then_inc(s_m1, 1)

    def ph_p4(st):
        ht = st.enter_context(nc.sbuf_tensor("ht", [128, 64, NSLOT], BF16))
        b2_bc = st.enter_context(nc.sbuf_tensor("b2_bc", [128, D], F32))
        u_f = [st.enter_context(nc.sbuf_tensor(f"u{i}", [128, NSLOT], F32)) for i in range(2)]
        g2 = [st.enter_context(nc.sbuf_tensor(f"g2_{i}", [128, NSLOT], F32)) for i in range(2)]
        g3 = [st.enter_context(nc.sbuf_tensor(f"g3_{i}", [128, NSLOT], F32)) for i in range(2)]
        tn = [st.enter_context(nc.sbuf_tensor(f"tn{i}", [128, NSLOT], F32)) for i in range(2)]

        with contextlib.ExitStack() as st2:
            w1s = [st2.enter_context(nc.sbuf_tensor(f"w1s{i}", [128, 16, 512], BF16)) for i in range(2)]

            @blk.gpsimd
            def _(g):
                g.wait_ge(s_pm, 68)
                g.dma_start(out=b2_bc[:], in_=b2.partition_broadcast(128)).then_inc(s_bb, 16)
                for j in range(16):
                    if j >= 2:
                        g.wait_ge(s_f1, 4 * (j - 1))
                    g.dma_start(out=w1s[j % 2][:],
                                in_=w1[:, j * 512:(j + 1) * 512]
                                .rearrange("(c p) d -> p c d", p=128)).then_inc(s_wl, 16)

            @blk.tensor
            def _(t):
                t.wait_ge(s_pa, 66)
                for j in range(16):
                    t.wait_ge(s_wl, 80 + 16 * (j + 1))
                    for m2 in range(4):
                        ix = j * 4 + m2
                        ub = pA[ix % 3][:, 0:NSLOT]
                        if ix >= 3:
                            t.wait_ge(s_f2, 2 * (ix - 3) + 1)
                        for c in range(16):
                            mm = t.matmul(out=ub, lhsT=w1s[j % 2][:, c, m2 * 128:(m2 + 1) * 128],
                                          rhs=late['xgt'][:, c, :], start=(c == 0), stop=(c == 15))
                        mm.then_inc(s_f1, 1)

            @blk.scalar
            def _(s):
                for ix in range(64):
                    s.wait_ge(s_f1, ix + 1)
                    if ix >= 2:
                        s.wait_ge(s_f3, 2 * ix - 2)
                    s.activation(out=u_f[ix % 2][:], in_=pA[ix % 3][:, 0:NSLOT],
                                 func=AF.Identity, bias=b1c[:, ix:ix + 1],
                                 scale=1.0).then_inc(s_f2, 1)
                    s.wait_ge(s_f3, 2 * ix + 1)
                    s.activation(out=tn[ix % 2][:], in_=tn[ix % 2][:], func=AF.Tanh,
                                 scale=0.7978845608028654).then_inc(s_f2, 1)

            @blk.vector
            def _(v):
                for ix in range(64):
                    v.wait_ge(s_f2, 2 * ix + 1)
                    u = u_f[ix % 2]
                    VF(v, v.tensor_tensor(out=g2[ix % 2][:], in0=u[:], in1=u[:], op=ALU.mult))
                    VF(v, v.tensor_tensor(out=g3[ix % 2][:], in0=g2[ix % 2][:], in1=u[:], op=ALU.mult))
                    v.scalar_tensor_tensor(out=tn[ix % 2][:], in0=g3[ix % 2][:],
                                           scalar=0.044715, in1=u[:],
                                           op0=ALU.mult, op1=ALU.add).then_inc(s_f3, 1)
                    v.wait_ge(s_f2, 2 * ix + 2)
                    VF(v, v.tensor_scalar(out=g2[ix % 2][:], in0=tn[ix % 2][:], scalar1=0.5,
                                          scalar2=0.5, op0=ALU.mult, op1=ALU.add))
                    v.tensor_tensor(out=ht[:, ix, :], in0=u[:], in1=g2[ix % 2][:],
                                    op=ALU.mult).then_inc(s_f3, 1)

        with contextlib.ExitStack() as st2:
            w2s = [st2.enter_context(nc.sbuf_tensor(f"w2s{i}", [128, 64, 256], BF16)) for i in range(2)]

            @blk.gpsimd
            def _(g):
                g.wait_ge(s_f1, 64)
                for j in range(8):
                    if j >= 2:
                        g.wait_ge(s_f4, 3 * (j - 1))
                    g.dma_start(out=w2s[j % 2][:],
                                in_=w2[:, j * 256:(j + 1) * 256]
                                .rearrange("(c p) d -> p c d", p=128)).then_inc(s_wl, 16)

            @blk.tensor
            def _(t):
                t.wait_ge(s_f3, 128)
                for j in range(8):
                    t.wait_ge(s_wl, 336 + 16 * (j + 1))
                    for r in range(3):
                        ix = j * 3 + r
                        vb = pA[ix % 3][:, 512:768]
                        if ix >= 3:
                            t.wait_ge(s_f5, ix - 2)
                        for c in range(64):
                            mm = t.matmul(out=vb, lhsT=ht[:, c, r * 128:(r + 1) * 128],
                                          rhs=w2s[j % 2][:, c, :], start=(c == 0),
                                          stop=(c == 63))
                        mm.then_inc(s_f4, 1)

            @blk.vector
            def _(v):
                v.wait_ge(s_m1, 8)
                v.wait_ge(s_bb, 16)
                for j in range(8):
                    for r in range(3):
                        ix = j * 3 + r
                        sl2 = slice(j * 256, (j + 1) * 256)
                        v.wait_ge(s_f4, ix + 1)
                        VF(v, v.tensor_tensor(out=tn[0][:, 0:256], in0=pA[ix % 3][:, 512:768],
                                              in1=b2_bc[:, sl2], op=ALU.add))
                        VF(v, v.tensor_scalar(out=tn[1][:, 0:256], in0=tn[0][:, 0:256],
                                              scalar1=late['gwcol'][:, r:r + 1],
                                              scalar2=None, op0=ALU.mult))
                        VF(v, v.tensor_copy(out=late['valsg'][:, r, sl2], in_=tn[1][:, 0:256]))
                        VF(v, v.tensor_copy(out=g2[0][:, 0:256], in_=late['valsg'][:, r, sl2]))
                        VF(v, v.tensor_tensor(out=g3[0][:, 0:256], in0=tn[1][:, 0:256],
                                              in1=g2[0][:, 0:256], op=ALU.subtract))
                        v.tensor_copy(out=late['valsl'][:, r, sl2],
                                      in_=g3[0][:, 0:256]).then_inc(s_f5, 1)

    def ph_p3(st):
        esb = st.enter_context(nc.sbuf_tensor("esb", [128, 16, 1024], BF16))
        qt = [st.enter_context(nc.sbuf_tensor(f"qt{i}", [128, SL], BF16)) for i in range(2)]
        kt = [st.enter_context(nc.sbuf_tensor(f"kt{i}", [128, S], BF16)) for i in range(2)]
        vt = [st.enter_context(nc.sbuf_tensor(f"vt{i}", [128, 32, 128], BF16)) for i in range(2)]
        rd_sb = st.enter_context(nc.sbuf_tensor("rd_sb", [1, 1024], F32))
        ftmp = st.enter_context(nc.sbuf_tensor("ftmp", [128, 512], BF16))
        den = st.enter_context(nc.psum_tensor("den", [128, 512], F32))

        @blk.sync
        def _(sync):
            sync.wait_ge(s_ks, 16 * 320)
            sync.wait_ge(s_f4, 24)
            for p in range(NPAIR):
                if p >= 2:
                    sync.wait_ge(s_av, (p - 1) * 16)
                sync.dma_start(out=qt[p % 2][:],
                               in_=qtd[p * 128:(p + 1) * 128, :]).then_inc(s_pl, 16)
                sync.dma_start(out=kt[p % 2][:],
                               in_=ktd[p * 128:(p + 1) * 128, :]).then_inc(s_pl, 16)
                for c in range(32):
                    sync.dma_start(out=vt[p % 2][:, c, :],
                                   in_=vtd[p * 128:(p + 1) * 128, c * 128:(c + 1) * 128],
                                   transpose=True).then_inc(s_pl, 16)

        def emit_ftmp(s, hb):
            s.wait_ge(s_av, (hb + 1) * 4)
            if hb > 0:
                s.wait_ge(s_fn, hb)
            s.copy(out=ftmp[:], in_=pB0[:]).then_inc(s_fe, 1)

        def emit_exps(s, p):
            for b in range(4):
                hb = p * 4 + b
                for q in range(4):
                    eg = hb * 4 + q
                    for j in range(8):
                        nsc = eg * 8 + j + 1
                        s.wait_ge(s_sc, nsc)
                        if j == 0 and eg > 1:
                            s.wait_ge(s_av, eg - 1)
                        s.activation(out=esb[:, (eg % 2) * 8 + j, :],
                                     in_=pA[(nsc - 1) % 2][:],
                                     func=AF.Exp, scale=SCALE).then_inc(s_ex, 1)
                        if q == 0 and j == 6 and hb > 0:
                            emit_ftmp(s, hb - 1)

        @blk.scalar
        def _(s):
            for p in range(NPAIR):
                emit_exps(s, p)
            emit_ftmp(s, 63)

        def ep_bcast(t, hb):
            t.wait_ge(s_rc, hb + 1)
            if hb >= 1:
                t.wait_ge(s_fn, hb)
            t.matmul(out=pA[2][0:64, 512:1024], lhsT=ones_rf[0:1, 0:64],
                     rhs=rd_sb[0:1, 0:512], start=True, stop=True,
                     tile_position=(0, 0), skip_group_check=True)
            t.matmul(out=pA[2][64:128, 512:1024], lhsT=ones_rf[0:1, 0:64],
                     rhs=rd_sb[0:1, 512:1024], start=True, stop=True,
                     tile_position=(0, 64),
                     skip_group_check=True).then_inc(s_bcm, 1)

        def phase2_chunk(t, gm, j):
            p, q = gm // 16, gm % 4
            dn = den if (gm // 4) % 2 == 0 else pA[2]
            c = q * 8 + j
            es = (gm % 2) * 8 + j
            first = (q == 0 and j == 0)
            last = (q == 3 and j == 7)
            t.matmul(out=pB0[0:64, :], lhsT=vt[p % 2][:, c, 0:64],
                     rhs=esb[:, es, 0:512], start=first, stop=last,
                     tile_position=(0, 0), skip_group_check=True)
            t.matmul(out=pB0[64:128, :], lhsT=vt[p % 2][:, c, 64:128],
                     rhs=esb[:, es, 512:1024], start=first, stop=last,
                     tile_position=(0, 64), skip_group_check=True)
            t.matmul(out=dn[0:1, 0:512], lhsT=ones_col[:],
                     rhs=esb[:, es, 0:512], start=first, stop=last,
                     tile_position=(0, 0), skip_group_check=True)
            mm = t.matmul(out=dn[64:65, 0:512], lhsT=ones_col[:],
                          rhs=esb[:, es, 512:1024], start=first, stop=last,
                          tile_position=(0, 64), skip_group_check=True)
            if j == 7:
                mm.then_inc(s_av, 1)

        @blk.tensor
        def _(t):
            for g in range(64 * 4):
                p, b, q = g // 16, (g // 4) % 4, g % 4
                hb = g // 4
                if g % 16 == 0:
                    t.wait_ge(s_pl, 34 * 16 * (p + 1))
                for j in range(8):
                    c = q * 8 + j
                    nsc = g * 8 + j + 1
                    t.matmul(out=den[32:33, 0:64], lhsT=ones_col[:],
                             rhs=kt[p % 2][:, 0:64], start=True,
                             stop=True, skip_group_check=True)
                    if q == 2 and j == 4 and hb > 0:
                        ep_bcast(t, hb - 1)
                    if nsc > 2:
                        t.wait_ge(s_ex, nsc - 2)
                    t.matmul(out=pA[(nsc - 1) % 2][:, 0:512],
                             lhsT=kt[p % 2][0:64, c * 128:(c + 1) * 128],
                             rhs=qt[p % 2][0:64, b * 512:(b + 1) * 512],
                             start=True, stop=True, tile_position=(0, 0))
                    t.matmul(out=pA[(nsc - 1) % 2][:, 512:1024],
                             lhsT=kt[p % 2][64:128, c * 128:(c + 1) * 128],
                             rhs=qt[p % 2][64:128, b * 512:(b + 1) * 512],
                             start=True, stop=True,
                             tile_position=(64, 0)).then_inc(s_sc, 1)
                    if g >= 1:
                        if (g - 1) % 4 == 0 and j == 0:
                            hbm = (g - 1) // 4
                            t.wait_ge(s_fe, hbm)
                            if hbm >= 2:
                                t.wait_ge(s_rc, hbm - 1)
                        phase2_chunk(t, g - 1, j)
            t.wait_ge(s_ex, 64 * 4 * 8)
            for j in range(8):
                phase2_chunk(t, 64 * 4 - 1, j)
            ep_bcast(t, 63)

        @blk.vector
        def _(v):
            for p in range(NPAIR):
                for b in range(4):
                    hb = p * 4 + b
                    v.wait_ge(s_av, (hb + 1) * 4)
                    if hb >= 1:
                        v.wait_ge(s_bcm, hb)
                    dn = den if hb % 2 == 0 else pA[2]
                    v.reciprocal(out=rd_sb[0:1, 0:512], in_=dn[0:1, 0:512])
                    v.reciprocal(out=rd_sb[0:1, 512:1024],
                                 in_=dn[64:65, 0:512]).then_inc(s_rc, 1)
                    v.wait_ge(s_bcm, hb + 1)
                    v.wait_ge(s_fe, hb + 1)
                    v.tensor_tensor(out=late['featsT'][:, p, b * 512:(b + 1) * 512],
                                    in0=ftmp[:], in1=pA[2][:, 512:1024],
                                    op=ALU.mult).then_inc(s_fn, 1)

    def ph_p5(st):
        wos = late["wos"]
        pt5 = st.enter_context(nc.sbuf_tensor("pt5", [128, 3, 16, 128], BF16))
        bo_bc = st.enter_context(nc.sbuf_tensor("bo_bc", [128, D], F32))
        xt2 = [st.enter_context(nc.sbuf_tensor(f"xt2_{i}", [128, 1024], F32)) for i in range(2)]
        atmp = st.enter_context(nc.sbuf_tensor("atmp", [128, 512], F32))
        stg = [st.enter_context(nc.sbuf_tensor(f"stg{i}", [128, 1024], F32)) for i in range(2)]

        for half in range(2):
            @blk.gpsimd
            def _(g, half=half):
                if half == 0:
                    g.wait_ge(s_f4, 24)
                    g.dma_start(out=wos[:],
                                in_=wo[:, 0:1024]
                                .rearrange("(c p) d -> p c d", p=128)).then_inc(s_wl, 16)
                else:
                    for n2 in range(2):
                        g.wait_ge(s_om, 31 + n2)
                        g.dma_start(out=wos[:, :, n2 * 512:(n2 + 1) * 512],
                                    in_=wo[:, 1024 + n2 * 512:1024 + (n2 + 1) * 512]
                                    .rearrange("(c p) d -> p c d", p=128)).then_inc(s_wl, 16)
                if half == 0:
                    g.wait_ge(s_bcm, 64)
                    g.dma_start(out=bo_bc[:], in_=bo.partition_broadcast(128)).then_inc(s_bb, 16)

            @blk.sync
            def _(sync, half=half):
                if half == 0:
                    sync.wait_ge(s_bcm, 64)
                    sync.wait_ge(s_fn, 64)
                    sync.wait_ge(s_ptd, 16)
                    sync.dma_start(out=pt5[:],
                                   in_=ptd.rearrange("p (a b c) -> p a b c",
                                                     a=3, b=16)).then_inc(s_ptd, 16)
                for m in range(16):
                    ix = half * 16 + m
                    if ix >= 2:
                        sync.wait_ge(s_oa, 2 * (ix - 1))
                    sync.dma_start(out=xt2[m % 2][:],
                                   in_=x[m * 128:(m + 1) * 128,
                                         half * 1024:(half + 1) * 1024]).then_inc(s_x2, 16)
                    if m >= 2:
                        sync.wait_ge(s_oa, 2 * (ix - 1))
                        sync.dma_start(out=out[(m - 2) * 128:(m - 1) * 128,
                                               half * 1024:(half + 1) * 1024],
                                       in_=stg[m % 2][:]).then_inc(s_os, 16)
                for m in range(14, 16):
                    ix = half * 16 + m
                    sync.wait_ge(s_oa, 2 * (ix + 1))
                    sync.dma_start(out=out[m * 128:(m + 1) * 128,
                                           half * 1024:(half + 1) * 1024],
                                   in_=stg[m % 2][:]).then_inc(s_os, 16)

            @blk.tensor
            def _(t, half=half):
                if half == 0:
                    t.wait_ge(s_wl, 480)
                    t.wait_ge(s_fn, 64)
                    t.wait_ge(s_f5, 24)
                    t.wait_ge(s_ptd, 32)
                for m in range(16):
                    for n2 in range(2):
                        if half == 1 and m == 0:
                            t.wait_ge(s_wl, 496 + 16 * n2)
                        ix = (half * 16 + m) * 2 + n2
                        ob = pA[ix % 3][:, 0:512]
                        if ix >= 3:
                            t.wait_ge(s_oa, ix - 2)
                        for pr in range(16):
                            t.matmul(out=ob, lhsT=late['featsT'][:, pr, m * 128:(m + 1) * 128],
                                     rhs=wos[:, pr, n2 * 512:(n2 + 1) * 512],
                                     start=(pr == 0), stop=False,
                                     skip_group_check=True)
                        for r in range(3):
                            t.matmul(out=ob, lhsT=pt5[:, r, m, :],
                                     rhs=late['valsg'][:, r, half * 1024 + n2 * 512:
                                               half * 1024 + (n2 + 1) * 512],
                                     start=False, stop=False,
                                     skip_group_check=True)
                        for r in range(3):
                            mm = t.matmul(out=ob, lhsT=pt5[:, r, m, :],
                                          rhs=late['valsl'][:, r, half * 1024 + n2 * 512:
                                                    half * 1024 + (n2 + 1) * 512],
                                          start=False, stop=(r == 2),
                                          skip_group_check=True)
                        mm.then_inc(s_om, 1)

            @blk.vector
            def _(v, half=half):
                if half == 0:
                    v.wait_ge(s_bb, 32)
                for m in range(16):
                    for n2 in range(2):
                        ix = (half * 16 + m) * 2 + n2
                        v.wait_ge(s_om, ix + 1)
                        v.wait_ge(s_x2, 16 * (half * 16 + m + 1))
                        if half * 16 + m >= 2:
                            v.wait_ge(s_os, 16 * (half * 16 + m - 1))
                        sl = slice(n2 * 512, (n2 + 1) * 512)
                        fsl = slice(half * 1024 + n2 * 512, half * 1024 + (n2 + 1) * 512)
                        v.tensor_scalar(out=atmp[:], in0=xt2[m % 2][:, sl],
                                        scalar1=a1_t[:, m:m + 1],
                                        scalar2=b1p_t[:, m:m + 1],
                                        op0=ALU.mult, op1=ALU.add)
                        VF(v, v.tensor_tensor(out=stg[m % 2][:, sl],
                                              in0=pA[ix % 3][:, 0:512],
                                              in1=bo_bc[:, fsl], op=ALU.add))
                        v.tensor_tensor(out=stg[m % 2][:, sl], in0=stg[m % 2][:, sl],
                                        in1=atmp[:], op=ALU.add).then_inc(s_oa, 1)


        @blk.sync
        def _(sync):
            sync.wait_ge(s_os, 16 * 32)

    with contextlib.ExitStack() as st, nc.named_scope("Pc"):
        ph_consts(st)
    with contextlib.ExitStack() as stw:
        stq = contextlib.ExitStack()
        early = dict(
            xnt=[stw.enter_context(nc.sbuf_tensor(f"xnt{i}", [128, 16, 512], BF16)) for i in range(2)],
            kst=[stw.enter_context(nc.sbuf_tensor(f"kst{i}", [128, 512], BF16)) for i in range(16)],
        )
        for nm, shp, dt in [("iota_i", [128, NSLOT], I32), ("iota384", [128, NSLOT], F32),
                            ("iota1r", [1, 16], F32), ("mrow", [1, 16], F32),
                            ("brow", [1, 18], F32), ("lor", [1, 1], F32),
                            ("hir", [1, 1], F32), ("hrow", [1, 1], F32),
                            ("t16a", [1, 16], F32), ("t16b", [1, 16], F32),
                            ("s1r", [1, 1], F32), ("s2r", [1, 1], F32),
                            ("q1r", [1, 1], F32), ("q2r", [1, 1], F32),
                            ("mbs", [128, 16], F32), ("cscr", [128, 16, 32], F32),
                            ("p16", [128, 16], F32), ("p16b", [128, 16], BF16)]:
            early[nm] = stw.enter_context(nc.sbuf_tensor(nm, shp, dt))
        early["wqs"] = stq.enter_context(nc.sbuf_tensor("wqs", [128, 16, 2048], BF16))
        early["stq"] = stq
        if upto >= 0:
            with contextlib.ExitStack() as st, nc.named_scope("P0"):
                ph_p0(st)
        if upto >= 1:
            with contextlib.ExitStack() as st, nc.named_scope("P2"):
                ph_p2(st, early)
    late["gwcol"] = A("gwcol", [128, 3], F32)
    late["valsg"] = A("valsg", [128, 3, D], BF16)
    late["valsl"] = A("valsl", [128, 3, D], BF16)
    stx = contextlib.ExitStack()
    late["xgt"] = stx.enter_context(nc.sbuf_tensor("xgt", [128, 16, NSLOT], BF16))
    stp = contextlib.ExitStack()
    late["pt_sb"] = stp.enter_context(nc.sbuf_tensor("pt_sb", [128, 3, 16, 128], BF16))
    if upto >= 2:
        with contextlib.ExitStack() as st, nc.named_scope("P2b"):
            ph_p2b(st, early)

        @blk.sync
        def _(sync):
            sync.wait_ge(s_pa, 50)
            sync.dma_start(out=ptd,
                           in_=late["pt_sb"].ap().rearrange("p a b c -> p (a b c)")
                           ).then_inc(s_ptd, 16)
    stp.close()
    if upto >= 3:
        with contextlib.ExitStack() as st, nc.named_scope("P4"):
            ph_p4(st)
        if dbg:
            xgtd = nc.dram_tensor("xgtd", [128, 16, NSLOT], BF16, kind="Internal").ap()

            @blk.sync
            def _(sync):
                sync.wait_ge(s_pa, 66)
                sync.dma_start(out=xgtd, in_=late["xgt"][:])
    stx.close()
    late["featsT"] = A("featsT", [128, 16, SL], BF16)
    late["wos"] = A("wos", [128, 16, 1024], BF16)
    if upto >= 4:
        with contextlib.ExitStack() as st, nc.named_scope("P3"):
            ph_p3(st)
    if upto >= 5:
        with contextlib.ExitStack() as st, nc.named_scope("P5"):
            ph_p5(st)

    if dbg:
        dоко = None
        dbg_done = SEM("s_dbg")
        dbgs = []

        def dump_dram(name, src_ap, shape, dt):
            o = nc.dram_tensor(name, shape, dt, kind="ExternalOutput").ap()
            dbgs.append((o, src_ap))

        @blk.sync
        def _(sync):
            if upto >= 0:
                sync.wait_ge(s_xnst, 16 * 32)
            if upto >= 1:
                sync.wait_ge(s_ks, 16 * 320)
            if upto >= 2:
                sync.wait_ge(s_m1, 8)
                sync.wait_ge(s_pa, 66)
            if upto >= 3:
                sync.wait_ge(s_f5, 24)
            if upto >= 4:
                sync.wait_ge(s_fn, 64)
            n = 0
            def dump(name, src_ap, shape, dt):
                nonlocal n
                o = nc.dram_tensor(name, shape, dt, kind="ExternalOutput").ap()
                sync.dma_start(out=o, in_=src_ap).then_inc(dbg_done, 16)
                n += 1
            if upto >= 0:
                dump("d_xn", xnd, [S, D], BF16)
                dump("d_w", w_all[:], [128, 32], F32)
                dump("d_mean", mean_all[:], [128, 32], F32)
                dump("d_rstd", rstd_all[:], [128, 32], F32)
            if upto >= 1:
                dump("d_qt", qtd, [D, SL], BF16)
                dump("d_kt", ktd, [D, S], BF16)
                dump("d_vt", vtd, [D, S], BF16)
            if upto >= 2:
                dump("d_hi", hi_c[:], [128, 1], F32)
                dump("d_mask", mask_t[:], [128, 16], F32)
                dump("d_rank", rank_t[:], [128, 16], F32)
                dump("d_xgt", xgtd, [128, 16, NSLOT], BF16)
                dump("d_gwc", late["gwcol"][:], [128, 3], F32)
                dump("d_pt", ptd, [128, 3, 16, 128], BF16)
            if upto >= 3:
                dump("d_vals", late["valsg"][:], [128, 3, D], BF16)
            if upto >= 4:
                dump("d_ft", late["featsT"][:], [128, 16, SL], BF16)
            sync.wait_ge(dbg_done, 16 * n)

    es.close()
    return nc


_NC_CACHE = {}


def _get_nc():
    if "nc" not in _NC_CACHE:
        _NC_CACHE["nc"] = build_nc()
    return _NC_CACHE["nc"]


def make_in_maps(L0_attn_x, L0_attn_Wq, L0_attn_bq, L0_attn_Wk, L0_attn_bk,
                 L0_attn_Wv, L0_attn_bv, L0_attn_Wo, L0_attn_bo,
                 L0_ffn_W_r, L0_ffn_b_r, L0_ffn_W1, L0_ffn_b1, L0_ffn_W2,
                 L0_ffn_b2, L0_ffn_position_ids=None, L0_ffn_cache_position=None,
                 **_unused):
    x = np.ascontiguousarray(np.asarray(L0_attn_x, dtype=np.float32))
    consts = dict(
        wq=np.ascontiguousarray(np.asarray(L0_attn_Wq, np.float32)),
        wk=np.ascontiguousarray(np.asarray(L0_attn_Wk, np.float32)),
        wv=np.ascontiguousarray(np.asarray(L0_attn_Wv, np.float32)),
        wo=np.ascontiguousarray(np.asarray(L0_attn_Wo, np.float32)),
        bq=np.ascontiguousarray(np.asarray(L0_attn_bq, np.float32).reshape(D)),
        bk=np.ascontiguousarray(np.asarray(L0_attn_bk, np.float32).reshape(D)),
        bv=np.ascontiguousarray(np.asarray(L0_attn_bv, np.float32).reshape(D)),
        bo=np.ascontiguousarray(np.asarray(L0_attn_bo, np.float32).reshape(D)),
        wr=np.ascontiguousarray(np.asarray(L0_ffn_W_r, np.float32).reshape(D, 1)),
        br=np.ascontiguousarray(np.asarray(L0_ffn_b_r, np.float32).reshape(1)),
        w1=np.ascontiguousarray(np.asarray(L0_ffn_W1, np.float32)),
        b1=np.ascontiguousarray(np.asarray(L0_ffn_b1, np.float32).reshape(DFF)),
        w2=np.ascontiguousarray(np.asarray(L0_ffn_W2, np.float32)),
        b2=np.ascontiguousarray(np.asarray(L0_ffn_b2, np.float32).reshape(D)),
    )
    in_maps = []
    for core in range(8):
        b, h = core // 2, core % 2
        m = dict(consts)
        m["x"] = np.ascontiguousarray(np.roll(x[b], -h * SL, axis=0))
        in_maps.append(m)
    return in_maps


def kernel(**inputs):
    in_maps = make_in_maps(**inputs)
    nc = _get_nc()
    res = run_bass_kernel_spmd(nc, in_maps, core_ids=list(range(8)))
    outp = np.empty((4, 4096, 2048), dtype=np.float32)
    for core in range(8):
        b, h = core // 2, core % 2
        outp[b, h * SL:(h + 1) * SL] = res.results[core]["out"]
    return outp
